# version 20
# baseline (speedup 1.0000x reference)
import numpy as np
import ml_dtypes
from contextlib import ExitStack

import concourse.bass as bass
import concourse.mybir as mybir
from concourse.alu_op_type import AluOpType as Alu
from concourse.bass_utils import run_bass_kernel_spmd

F32 = mybir.dt.float32
F32R = mybir.dt.float32r
DBG_B3 = False
BF16 = mybir.dt.bfloat16
I8 = mybir.dt.int8
I32 = mybir.dt.int32
U8 = mybir.dt.uint8
ACT = mybir.ActivationFunctionType
AXX = mybir.AxisListType.X

MAGIC = 3.0 * 2.0 ** 22
NEPS = 1e-6
QS_SCALE = 0.5 * (128.0 ** -0.5)
ISQ = float(1.0 / np.float32(np.sqrt(np.float32(128.0))))


class Cfg:
    def __init__(self, N=2048, D=2048, H=16, DFF=8192):
        self.N, self.D, self.H, self.DFF = N, D, H, DFF
        self.DH = 128
        assert self.H * self.DH == D
        self.KD = D // 128
        self.TT = N // 128
        self.NS = N // 512
        self.NCH = N // 128
        self.KF = DFF // 128
        self.FD = DFF // 512
        self.G8 = N // 256
        assert N % 512 == 0 and DFF % 2048 == 0 and D % 512 == 0
        assert self.NCH % 4 == 0 and self.KD % 4 == 0 and self.KF % 4 == 0


def _consts(cfg):
    c = {}
    c["eye_bf"] = np.eye(128, dtype=ml_dtypes.bfloat16)
    c["eye_f"] = np.eye(128, dtype=np.float32)
    i = np.arange(128)
    c["mask_l_neg"] = np.where(i[:, None] > i[None, :], -1.0, 0.0
                               ).astype(ml_dtypes.bfloat16)
    c["mask_u_neg"] = np.where(i[:, None] < i[None, :], -1.0, 0.0
                               ).astype(ml_dtypes.bfloat16)
    c["mask_u_incl"] = (i[:, None] <= i[None, :]).astype(ml_dtypes.bfloat16)
    f = np.arange(512)
    c["dmask"] = np.stack([((i[:, None] + 128 * j) <= f[None, :])
                           .astype(ml_dtypes.bfloat16) for j in range(4)])
    c["ones_col"] = np.ones((1, 128), np.float32)
    c["ones_col_bf"] = np.ones((1, 128), ml_dtypes.bfloat16)
    c["ones128"] = np.ones((128, 1), np.float32)
    c["ones128_bf"] = np.ones((128, 1), ml_dtypes.bfloat16)
    c["scol"] = np.full((128, 1), 1.0 / QS_SCALE, np.float32)
    return c


def build_program(cfg: Cfg, stop_after=None):
    nc = bass.Bass("TRN2", target_bir_lowering=False)
    N, D, H, DFF = cfg.N, cfg.D, cfg.H, cfg.DFF
    KD, TT, NS, NCH, KF, FD, G8 = (cfg.KD, cfg.TT, cfg.NS, cfg.NCH,
                                   cfg.KF, cfg.FD, cfg.G8)

    def di(name, shape, dt=F32):
        return nc.dram_tensor(name, shape, dt, kind="ExternalInput").ap()

    x_in = di("x_in", [N, D])
    Wq = di("Wq", [D, D]); Wk = di("Wk", [D, D]); Wv = di("Wv", [D, D])
    Wo = di("Wo", [D, D])
    lnw = di("lnw", [1, D]); lnb = di("lnb", [1, D])
    W1 = di("W1", [DFF, D]); b1 = di("b1", [1, DFF])
    W2 = di("W2", [D, DFF]); b2 = di("b2", [1, D])
    eye_bf = di("eye_bf", [128, 128], BF16)
    eye_f = di("eye_f", [128, 128])
    mask_l_neg = di("mask_l_neg", [128, 128], BF16)
    mask_u_neg = di("mask_u_neg", [128, 128], BF16)
    mask_u_incl = di("mask_u_incl", [128, 128], BF16)
    dmask = di("dmask", [4, 128, 512], BF16)
    ones_col = di("ones_col", [1, 128])
    ones_col_bf = di("ones_col_bf", [1, 128], BF16)
    ones128 = di("ones128", [128, 1])
    ones128_bf = di("ones128_bf", [128, 1], BF16)
    scol = di("scol", [128, 1])

    y_out = nc.dram_tensor("y_out", [N, D], F32, kind="ExternalOutput").ap()

    def dsc(name, shape, dt):
        return nc.dram_tensor(name, shape, dt, kind="Internal").ap()

    qraw_s = dsc("qraw_s", [H, 128, N], BF16)
    kraw_s = dsc("kraw_s", [H, 128, N], BF16)
    sig_s = dsc("sig_s", [H, 128, N], BF16)
    vraw_s = dsc("vraw_s", [N, D], BF16)
    wobf_s = dsc("wobf_s", [D, D], BF16)
    bound_s = dsc("bound_s", [H, 1, N], BF16)
    gqT_s = dsc("gqT_s", [DFF, N], BF16)
    w1qt_s = dsc("w1qt_s", [D, DFF], BF16)
    w2qt_s = dsc("w2qt_s", [DFF, D], BF16)
    dbg1 = dsc("dbg1", [128, 512], F32)
    dbg2 = dsc("dbg2", [128, 2048], BF16)

    st = ExitStack()
    A = nc.alloc_sbuf_tensor

    t_eye_bf = A("t_eye_bf", [128, 128], BF16)
    t_eye_f = A("t_eye_f", [128, 128], F32)
    t_eye_r = A("t_eye_r", [128, 128], F32R)
    t_mlneg = A("t_mlneg", [128, 128], BF16)
    t_muneg = A("t_muneg", [128, 128], BF16)
    t_muincl = A("t_muincl", [128, 128], BF16)
    t_ones_col = A("t_ones_col", [1, 128], F32)
    t_ones_col_bf = A("t_ones_col_bf", [1, 128], BF16)
    t_ones128 = A("t_ones128", [128, 1], F32)
    t_ones128_bf = A("t_ones128_bf", [128, 1], BF16)
    t_scol = A("t_scol", [128, 1], F32)

    QB = 64
    t_amax = A("t_amax", [128, QB], F32)
    t_bits = A("t_bits", [128, QB], I32)
    t_ef = A("t_ef", [128, QB], F32)
    t_sh = A("t_sh", [128, QB], I32)
    t_c23 = A("t_c23", [128, QB], I32)
    t_scl = A("t_scl", [128, QB], F32)
    t_inv = A("t_inv", [128, QB], F32)

    t_w0 = A("t_w0", [128, 2048], F32)
    t_w1 = A("t_w1", [128, 2048], F32)
    t_w2 = A("t_w2", [128, 2048], F32)
    t_bfa = A("t_bfa", [128, 2048], BF16)
    t_bfb = A("t_bfb", [128, 2048], BF16)
    t_i8a = A("t_i8a", [128, 2048], I8)
    t_rows = A("t_rows", [1, 512], F32)

    t_wT = A("t_wT", [128, 128], F32R)
    t_up = A("t_up", [128, 128], F32R)
    t_S = A("t_S", [128, 128], F32R)
    t_km = A("t_km", [1, 1], F32)
    t_stats = A("t_stats", [128, 24], F32)
    t_mv = A("t_mv", [128, 2], F32)
    t_sv1 = A("t_sv1", [128, 1], F32)
    t_sv2 = A("t_sv2", [128, 1], F32)
    t_sv3 = A("t_sv3", [128, 1], F32)
    t_sv4 = A("t_sv4", [128, 1], F32)

    REG_A = A("REG_A", [128, 65536], U8)
    REG_B = A("REG_B", [128, 81920], U8)
    REG_R = A("REG_R", [128, 20480], U8)

    def vA(byte_off, dt, cols):
        e = mybir.dt.size(dt)
        assert byte_off % e == 0
        return REG_A[:].bitcast(dt)[:, byte_off // e: byte_off // e + cols]

    def vB(byte_off, dt, cols):
        e = mybir.dt.size(dt)
        assert byte_off % e == 0
        return REG_B[:].bitcast(dt)[:, byte_off // e: byte_off // e + cols]

    def vR(byte_off, cols):
        return REG_R[:].bitcast(F32R)[:, byte_off // 4: byte_off // 4 + cols]

    def xfm(k):
        return vA(2 * k * N, BF16, N)

    def attnpre(k):
        return vA(2 * k * N, BF16, N)

    def tqT(k):
        return vA(2 * k * N, BF16, N)

    def gqT8b(b):
        return vA(b * (KF * 512), BF16, KF * 256)

    def wres(k):
        return vB(2 * k * D, BF16, D)

    def h1(t):
        return vB(2 * t * D, BF16, D)

    def w2qd(j):
        return vB(2 * j * 2048, BF16, 2048)

    BSTG = max(2 * KD * D, 8 * 4096)

    def bstage():
        return vB(BSTG, BF16, 2048)

    def t_asmt(b=0):
        return vB(BSTG + 4096 + b * 1024, BF16, 512)

    o_qraw, o_kraw, o_vraw = 0, 2 * N, 4 * N
    o_beta = 6 * N
    o_btg = 10 * N
    o_dmask = 10 * N + 2048
    o_row = o_dmask + 4096
    o_rowb = o_row + 4 * N
    o_qsb = o_rowb + 4 * N
    o_kcb = o_qsb + 2 * N
    o_rowc = o_kcb + 2 * N
    o_rowd = o_rowc + 2 * N
    assert o_rowd + 1024 <= 81920

    def hQRAW(): return vB(o_qraw, BF16, N)
    def hKRAW(): return vB(o_kraw, BF16, N)
    def hVRAW(): return vB(o_vraw, BF16, N)
    def hBETA(): return vB(o_beta, F32, N)
    def dBTG(): return vB(o_btg, F32, 512)
    def t_dmaskv(): return vB(o_dmask, BF16, 2048)
    def hROW(): return vB(o_row, F32, N)
    def hROWB(): return vB(o_rowb, F32, N)
    def hQSB(): return vB(o_qsb, BF16, N)
    def hKCB(): return vB(o_kcb, BF16, N)
    def hROWC(): return vB(o_rowc, BF16, N)
    def hROWD(): return vB(o_rowd, BF16, 512)

    TAIL = 2 * TT * D

    def lnw_rep(): return vB(TAIL, BF16, D)
    def lnb_rep(): return vB(TAIL + 2 * D, BF16, D)
    def wo_slab(b): return vB(TAIL + 4 * D + 2 * b * D, BF16, D)
    assert TAIL + 8 * D <= 81920

    def w1slab(): return vB(TAIL, BF16, KD * 512)
    assert TAIL + 2 * KD * 512 <= 81920

    def w2slab(b): return vB(TAIL + 2 * b * D, BF16, D)

    def b2row(): return vB(TAIL + 4096, F32, D)

    r_gqs, r_gkc = 0, 2048
    r_Pa, r_Pb = 4096, 8192
    r_Q = 12288
    r_kbg = r_Q + 4096
    r_kctm = r_kbg + 2048
    assert r_kctm + 2048 <= 20480

    def gQS(): return vR(r_gqs, 512)
    def gKC(): return vR(r_gkc, 512)
    def dPa(): return vR(r_Pa, 1024)
    def dPb(): return vR(r_Pb, 1024)
    def dQ(i): return vR(r_Q + 2048 * i, 512)
    def dKBG(): return vR(r_kbg, 512)
    def dKCTM(): return vR(r_kctm, 512)

    PSn = [nc.alloc_psum_tensor(f"ps{i}", [128, 512], F32) for i in range(8)]

    sems = {n: st.enter_context(nc.semaphore(n))
            for n in ["ldc", "ld0", "ld1", "ld2", "ld3", "ldh",
                      "mm", "vec", "sca", "gpA", "gpB", "gpC", "gpD", "out",
                      "poo"]}

    class Ctr:
        __slots__ = ("h", "n")

        def __init__(self, name):
            self.h = sems[name]; self.n = 0

        def inc(self, k=1):
            self.n += k
            return self.n
    ldc = Ctr("ldc"); ld0 = Ctr("ld0"); ld1 = Ctr("ld1")
    ld2 = Ctr("ld2"); ld3 = Ctr("ld3"); ldh = Ctr("ldh")
    mm = Ctr("mm"); vec = Ctr("vec"); sca = Ctr("sca")
    gpA = Ctr("gpA"); gpB = Ctr("gpB"); gpC = Ctr("gpC"); gpD = Ctr("gpD")
    out_c = Ctr("out")
    poo = Ctr("poo")
    ldpp = [ld0, ld1]
    gpl = [gpA, gpB]
    gsl = [gpC, gpD]

    def gp_all():
        return [(gpA, gpA.n), (gpB, gpB.n), (gpC, gpC.n), (gpD, gpD.n)]

    class Bank:
        def __init__(self, idx):
            self.t = PSn[idx]
            self.free_at = []

        def ap(self, cols=512, off=0):
            return self.t[:, off:off + cols]

        def apt(self, dt, cols=128, off_f32=0):
            e = 4 // mybir.dt.size(dt)
            return self.t[:].bitcast(dt)[:, off_f32 * e: off_f32 * e + cols]
    banks = [Bank(i) for i in range(8)]

    ops = None
    ENG = {"sync": lambda: nc.sync, "vector": lambda: nc.vector,
           "scalar": lambda: nc.scalar, "tensor": lambda: nc.tensor,
           "gpsimd": lambda: nc.gpsimd}
    dirty = {e: set() for e in ENG}

    def _nm(x):
        return x.name if hasattr(x, "name") else str(x)

    def em(eng, method, *args, wait=(), inc=None, reads=(), writes=(), **kw):
        need_drain = any(_nm(r) in dirty[eng]
                         for r in tuple(reads) + tuple(writes))
        if need_drain:
            dirty[eng].clear()
        for w_ in writes:
            dirty[eng].add(_nm(w_))
        waits = [((c.h if isinstance(c, Ctr) else c), v) for (c, v) in wait]
        ih = None
        if inc is not None:
            ctr, k = inc
            ih = (ctr.h, k)
            ctr.inc(k)

        def f():
            e = ENG[eng]()
            if need_drain:
                e.drain()
            for (h_, v_) in waits:
                e.wait_ge(h_, v_)
            ins = getattr(e, method)(*args, **kw)
            if ih is not None:
                ins.then_inc(ih[0], ih[1])
        ops[eng].append(f)

    def ewait(eng, *pairs):
        prs = [((c.h if isinstance(c, Ctr) else c), v) for (c, v) in pairs]

        def f():
            e = ENG[eng]()
            for h_, v_ in prs:
                e.wait_ge(h_, v_)
        ops[eng].append(f)

    def run_block():
        nonlocal ops
        myops = ops

        with nc.Block() as blk:
            @blk.sync
            def _(e):
                for f in myops["sync"]:
                    f()

            @blk.vector
            def _(e):
                for f in myops["vector"]:
                    f()

            @blk.scalar
            def _(e):
                for f in myops["scalar"]:
                    f()

            @blk.tensor
            def _(e):
                for f in myops["tensor"]:
                    f()

            @blk.gpsimd
            def _(e):
                for f in myops["gpsimd"]:
                    f()
        ops = None
        for e in ENG:
            dirty[e].clear()
        for b in banks:
            b.free_at = []

    def new_block():
        nonlocal ops
        ops = {e: [] for e in ENG}
        for e in ENG:
            dirty[e].clear()

    def hnd(ap_or_t):
        return ap_or_t.tensor if isinstance(ap_or_t, bass.AP) else ap_or_t

    def quant_tile(src_ap, n_cols, out_bf_ap, *, inc=None, wait_src=(),
                   src_reads=(), dst_wait=()):
        nb = n_cols // 32
        src3 = src_ap.rearrange("p (b k) -> p b k", k=32)
        sreads = tuple(hnd(r) for r in src_reads)
        em("vector", "tensor_reduce", out=t_amax[:, 0:nb], in_=src3, axis=AXX,
           op=Alu.max, apply_absolute_value=True, wait=wait_src,
           reads=sreads, writes=(t_amax,))
        em("vector", "tensor_scalar", out=t_amax[:, 0:nb], in0=t_amax[:, 0:nb],
           scalar1=2.0 ** -30, scalar2=None, op0=Alu.max,
           reads=(t_amax,), writes=(t_amax,))
        em("vector", "tensor_tensor", out=t_bits[:, 0:nb],
           in0=t_amax[:, 0:nb].bitcast(I32), in1=t_c23[:, 0:nb],
           op=Alu.logical_shift_right, reads=(t_amax,), writes=(t_bits,))
        em("vector", "tensor_copy", out=t_ef[:, 0:nb], in_=t_bits[:, 0:nb],
           reads=(t_bits,), writes=(t_ef,))
        em("vector", "tensor_scalar", out=t_sh[:, 0:nb], in0=t_ef[:, 0:nb],
           scalar1=-6.0, scalar2=None, op0=Alu.add, reads=(t_ef,),
           writes=(t_sh,))
        em("vector", "tensor_tensor", out=t_sh[:, 0:nb], in0=t_sh[:, 0:nb],
           in1=t_c23[:, 0:nb], op=Alu.logical_shift_left,
           reads=(t_sh,), writes=(t_sh,))
        em("vector", "tensor_copy", out=t_scl[:, 0:nb],
           in_=t_sh[:, 0:nb].bitcast(F32), reads=(t_sh,), writes=(t_scl,))
        em("vector", "tensor_scalar", out=t_sh[:, 0:nb], in0=t_ef[:, 0:nb],
           scalar1=-1.0, scalar2=260.0, op0=Alu.mult, op1=Alu.add,
           reads=(t_ef, t_scl), writes=(t_sh,))
        em("vector", "tensor_tensor", out=t_sh[:, 0:nb], in0=t_sh[:, 0:nb],
           in1=t_c23[:, 0:nb], op=Alu.logical_shift_left,
           reads=(t_sh,), writes=(t_sh,))
        em("vector", "tensor_copy", out=t_inv[:, 0:nb],
           in_=t_sh[:, 0:nb].bitcast(F32), reads=(t_sh,), writes=(t_inv,))
        invb = t_inv[:, 0:nb].unsqueeze(2).broadcast_to([128, nb, 32])
        sclb = t_scl[:, 0:nb].unsqueeze(2).broadcast_to([128, nb, 32])
        em("vector", "tensor_tensor",
           out=t_w2[:, 0:n_cols].rearrange("p (b k) -> p b k", k=32),
           in0=src3, in1=invb, op=Alu.mult,
           reads=(t_inv,) + sreads, writes=(t_w2,))
        em("vector", "tensor_scalar", out=t_w2[:, 0:n_cols],
           in0=t_w2[:, 0:n_cols], scalar1=MAGIC, scalar2=MAGIC,
           op0=Alu.add, op1=Alu.subtract, reads=(t_w2,), writes=(t_w2,))
        em("vector", "scalar_tensor_tensor",
           out=out_bf_ap.rearrange("p (b k) -> p b k", k=32),
           in0=t_w2[:, 0:n_cols].rearrange("p (b k) -> p b k", k=32),
           scalar=127.0, in1=sclb, op0=Alu.min, op1=Alu.mult,
           wait=list(dst_wait),
           inc=inc, reads=(t_scl,), writes=(hnd(out_bf_ap), t_w2))

    def rstd_chain(var_ap, out_ap):
        em("vector", "tensor_scalar", out=t_sv1[:], in0=var_ap, scalar1=1e-5,
           scalar2=None, op0=Alu.add, reads=(hnd(var_ap),), writes=(t_sv1,),
           inc=(vec, 1))
        em("scalar", "activation", out=t_sv2[:], in_=t_sv1[:], func=ACT.Ln,
           wait=[(vec, vec.n)], reads=(t_sv1,), writes=(t_sv2,))
        em("scalar", "activation", out=out_ap, in_=t_sv2[:], func=ACT.Exp,
           scale=-0.5, reads=(t_sv2,), writes=(hnd(out_ap),), inc=(sca, 1))
        em("vector", "tensor_tensor", out=t_sv3[:], in0=out_ap, in1=out_ap,
           op=Alu.mult, wait=[(sca, sca.n)], reads=(hnd(out_ap), t_sv1),
           writes=(t_sv3,))
        em("vector", "tensor_tensor", out=t_sv3[:], in0=t_sv3[:], in1=t_sv1[:],
           op=Alu.mult, reads=(t_sv3,), writes=(t_sv3,))
        em("vector", "tensor_scalar", out=t_sv3[:], in0=t_sv3[:], scalar1=-0.5,
           scalar2=1.5, op0=Alu.mult, op1=Alu.add, reads=(t_sv3,),
           writes=(t_sv3,))
        em("vector", "tensor_tensor", out=out_ap, in0=out_ap, in1=t_sv3[:],
           op=Alu.mult, reads=(t_sv3, hnd(out_ap)), writes=(hnd(out_ap),))

    def layer_norm_apply(src_ap, dst_ap, inc=None, extra_reads=(),
                         wait_src=(), act_wait=()):
        npc = D // 512
        for j in range(npc):
            em("vector", "bn_stats", out=t_stats[:, 6 * j:6 * j + 6],
               in_=src_ap[:, 512 * j:512 * (j + 1)],
               wait=(wait_src if j == 0 else ()),
               reads=(hnd(src_ap),), writes=(t_stats,))
        em("vector", "bn_aggr", out=t_mv[:], in_=t_stats[:, 0:6 * npc],
           reads=(t_stats,), writes=(t_mv,))
        rstd_chain(t_mv[:, 1:2], t_sv4[:])
        em("vector", "tensor_tensor", out=t_sv1[:], in0=t_mv[:, 0:1],
           in1=t_sv4[:], op=Alu.mult, reads=(t_sv4, t_mv), writes=(t_sv1,))
        em("vector", "tensor_scalar", out=t_sv1[:], in0=t_sv1[:], scalar1=-1.0,
           scalar2=None, op0=Alu.mult, reads=(t_sv1,), writes=(t_sv1,),
           inc=(vec, 1))
        em("scalar", "activation", out=dst_ap, in_=src_ap, func=ACT.Identity,
           scale=t_sv4[:], bias=t_sv1[:], wait=[(vec, vec.n)] + list(act_wait),
           reads=(t_sv4, t_sv1) + tuple(hnd(r) for r in extra_reads),
           writes=(hnd(dst_ap),), inc=inc)

    new_block()
    for dst, src in [(t_eye_bf[:], eye_bf), (t_eye_f[:], eye_f),
                     (t_mlneg[:], mask_l_neg), (t_muneg[:], mask_u_neg),
                     (t_muincl[:], mask_u_incl),
                     (t_ones_col[:], ones_col),
                     (t_ones_col_bf[:], ones_col_bf),
                     (t_ones128[:], ones128),
                     (t_ones128_bf[:], ones128_bf), (t_scol[:], scol)]:
        em("sync", "dma_start", out=dst, in_=src, inc=(ldc, 16))
    em("vector", "memset", t_c23[:], 23)
    em("scalar", "copy", out=t_eye_r[:], in_=t_eye_f[:],
       wait=[(ldc, ldc.n)], inc=(sca, 1))
    bufm = [[(sca, sca.n)], []]
    bfm = [[], []]
    for t in range(TT):
        b = t % 2
        buf = t_w0 if b == 0 else t_w1
        bfb = t_bfa if b == 0 else t_bfb
        em("sync", "dma_start", out=buf[:, 0:D],
           in_=x_in[t * 128:(t + 1) * 128, :], wait=bufm[b],
           inc=(ldpp[b], 16))
        em("vector", "tensor_copy", out=bfb[:, 0:D], in_=buf[:, 0:D],
           wait=[(ldpp[b], ldpp[b].n)] + bfm[b], inc=(vec, 1),
           reads=(buf,), writes=(bfb,))
        bufm[b] = [(vec, vec.n)]
        vc = vec.n
        for k in range(KD):
            bk = banks[4 + (k % 2)]
            em("tensor", "transpose", bk.apt(BF16),
               bfb[:, k * 128:(k + 1) * 128], t_eye_bf[:],
               wait=[(vec, vc)] + ([(ldc, ldc.n)] if t == 0 and k == 0
                                   else []) + bk.free_at, inc=(mm, 1))
            em("scalar", "copy", out=xfm(k)[:, t * 128:(t + 1) * 128],
               in_=bk.apt(BF16), wait=[(mm, mm.n)], inc=(sca, 1))
            bk.free_at = [(sca, sca.n)]
        bfm[b] = [(mm, mm.n)]
    run_block()
    if stop_after == 'B0':
        st.close()
        return nc

    new_block()
    wbm = [[], []]

    def load_w_to_wres(Wsrc):
        for k in range(KD):
            b = k % 2
            buf = t_w0 if b == 0 else t_w1
            em("sync", "dma_start", out=buf[:, 0:D],
               in_=Wsrc[k * 128:(k + 1) * 128, :], wait=wbm[b],
               inc=(ldpp[b], 16))
            em("vector", "tensor_copy", out=wres(k), in_=buf[:, 0:D],
               wait=[(ldpp[b], ldpp[b].n)], inc=(vec, 1),
               reads=(buf,), writes=(REG_B,))
            wbm[b] = [(vec, vec.n)]
        return vec.n

    _b1list = [(Wq, qraw_s, False)]
    if stop_after != "B1q":
        _b1list.append((Wk, kraw_s, True))
    for (Wsrc, dst_s, with_sig) in _b1list:
        wdone = load_w_to_wres(Wsrc)
        for h in range(H):
            for s in range(NS):
                seq = h * NS + s
                pb = seq % 2
                bk = banks[pb]
                for k in range(KD):
                    em("tensor", "matmul", bk.ap(512),
                       wres(k)[:, h * 128:(h + 1) * 128],
                       xfm(k)[:, s * 512:(s + 1) * 512],
                       wait=(([(vec, wdone)] if (h == 0 and s == 0 and k == 0)
                              else []) + (bk.free_at if k == 0 else [])),
                       start=(k == 0), stop=(k == KD - 1),
                       inc=(mm, 1) if k == KD - 1 else None)
                ev = bstage()[:, pb * 512:pb * 512 + 512]
                em("vector", "tensor_copy", out=ev, in_=bk.ap(512),
                   wait=[(mm, mm.n), (gpl[pb], gpl[pb].n)], inc=(vec, 1),
                   reads=(), writes=(REG_B,))
                em("gpsimd", "dma_start",
                   out=dst_s[h, :, s * 512:(s + 1) * 512], in_=ev,
                   wait=[(vec, vec.n)], inc=(gpl[pb], 16))
                if with_sig:
                    sg = t_bfb[:, pb * 512:pb * 512 + 512]
                    em("scalar", "activation", out=sg, in_=bk.ap(512),
                       func=ACT.Sigmoid,
                       wait=[(vec, vec.n), (gsl[pb], gsl[pb].n)],
                       inc=(sca, 1), reads=(), writes=(t_bfb,))
                    em("gpsimd", "dma_start",
                       out=sig_s[h, :, s * 512:(s + 1) * 512], in_=sg,
                       wait=[(sca, sca.n)], inc=(gsl[pb], 16))
                    bk.free_at = [(vec, vec.n), (sca, sca.n)]
                else:
                    bk.free_at = [(vec, vec.n)]
    if stop_after in ("B1q", "B1k"):
        run_block()
        st.close()
        return nc
    wdone = load_w_to_wres(Wv)
    ND4 = D // 512
    for t in range(TT):
        bks = [banks[4 + j] for j in range(ND4)]
        for k in range(KD):
            for j in range(ND4):
                em("tensor", "matmul", bks[j].ap(512),
                   xfm(k)[:, t * 128:(t + 1) * 128],
                   wres(k)[:, j * 512:(j + 1) * 512],
                   wait=(([(vec, wdone)] if (t == 0 and k == 0 and j == 0)
                          else []) + (bks[j].free_at if k == 0 else [])),
                   start=(k == 0), stop=(k == KD - 1),
                   inc=(mm, 1) if (k == KD - 1 and j == ND4 - 1) else None)
        for j in range(ND4):
            em("vector", "tensor_copy",
               out=bstage()[:, j * 512:(j + 1) * 512], in_=bks[j].ap(512),
               wait=[(mm, mm.n)] +
                    ([(gpA, gpA.n)] if j == 0 else []) +
                    ([(gpB, gpB.n), (gpC, gpC.n), (gpD, gpD.n)]
                     if t == 0 and j == 0 else []),
               inc=(vec, 1) if j == ND4 - 1 else None,
               reads=(), writes=(REG_B,))
        for j in range(ND4):
            bks[j].free_at = [(vec, vec.n)]
        em("gpsimd", "dma_start", out=vraw_s[t * 128:(t + 1) * 128, :],
           in_=bstage()[:, 0:D], wait=[(vec, vec.n)], inc=(gpA, 16))
    run_block()
    if stop_after == 'B1':
        st.close()
        return nc

    new_block()
    wbm = [[], []]
    for k in range(KD):
        b = k % 2
        buf = t_w0 if b == 0 else t_w1
        em("sync", "dma_start", out=buf[:, 0:D],
           in_=Wo[k * 128:(k + 1) * 128, :], wait=wbm[b], inc=(ldpp[b], 16))
        dstc = bstage()[:, 0:D] if b == 0 else t_bfa[:, 0:D]
        em("vector", "tensor_copy", out=dstc, in_=buf[:, 0:D],
           wait=[(ldpp[b], ldpp[b].n), (gpl[b], gpl[b].n)], inc=(vec, 1),
           reads=(buf,), writes=(REG_B if b == 0 else t_bfa,))
        wbm[b] = [(vec, vec.n)]
        em("gpsimd", "dma_start", out=wobf_s[k * 128:(k + 1) * 128, :],
           in_=dstc, wait=[(vec, vec.n)], inc=(gpl[b], 16))
    setm = [[], []]
    for wg in range(KF // 4):
        sset = wg % 2
        qd = []
        for wi in range(4):
            wt = wg * 4 + wi
            b = wi % 2
            buf = t_w0 if b == 0 else t_w1
            em("sync", "dma_start", out=buf[:, 0:D],
               in_=W1[wt * 128:(wt + 1) * 128, :], wait=wbm[b],
               inc=(ldpp[b], 16))
            dq = w2qd(4 * sset + wi)[:, 0:D]
            quant_tile(buf[:, 0:D], D, dq, inc=(poo, 1),
                       wait_src=[(ldpp[b], ldpp[b].n)] +
                                (setm[sset] if wi == 0 else []) +
                                (gp_all() if wg == 0 and wi == 0 else []),
                       src_reads=(buf,))
            wbm[b] = [(poo, poo.n)]
            qd.append(dq)
        qdone = poo.n
        for k in range(KD):
            ab = k % 2
            lane = gsl[ab]
            for wi in range(4):
                bk = banks[4 + wi % 2]
                em("tensor", "transpose", bk.apt(BF16),
                   qd[wi][:, k * 128:(k + 1) * 128], t_eye_bf[:],
                   wait=([(poo, qdone)] if (k == 0 and wi < 2) else []) +
                        bk.free_at, inc=(mm, 1))
                em("scalar", "copy",
                   out=t_asmt(ab)[:, wi * 128:(wi + 1) * 128],
                   in_=bk.apt(BF16),
                   wait=[(mm, mm.n)] + ([(lane, lane.n)]
                                        if wi == 0 else []),
                   inc=(sca, 1), reads=(), writes=(REG_B,))
                bk.free_at = [(sca, sca.n)]
            em("gpsimd", "dma_start",
               out=w1qt_s[k * 128:(k + 1) * 128, wg * 512:(wg + 1) * 512],
               in_=t_asmt(ab)[:, 0:512], wait=[(sca, sca.n)], inc=(lane, 16))
        setm[sset] = [(mm, mm.n)]
    NW2C = DFF // 2048
    wgi = 1
    for wg in range(KD // 4):
        for ch in range(NW2C):
            sset = wgi % 2
            wgi += 1
            qd = []
            for wi in range(4):
                wt = wg * 4 + wi
                b = wi % 2
                buf = t_w0 if b == 0 else t_w1
                em("sync", "dma_start", out=buf[:, 0:2048],
                   in_=W2[wt * 128:(wt + 1) * 128,
                          ch * 2048:(ch + 1) * 2048],
                   wait=wbm[b], inc=(ldpp[b], 16))
                dq = w2qd(4 * sset + wi)[:, 0:2048]
                quant_tile(buf[:, 0:2048], 2048, dq, inc=(poo, 1),
                           wait_src=[(ldpp[b], ldpp[b].n)] +
                                    (setm[sset] if wi == 0 else []),
                           src_reads=(buf,))
                wbm[b] = [(poo, poo.n)]
                qd.append(dq)
            qdone = poo.n
            for kk in range(16):
                ab = kk % 2
                lane = gsl[ab]
                for wi in range(4):
                    bk = banks[4 + wi % 2]
                    em("tensor", "transpose", bk.apt(BF16),
                       qd[wi][:, kk * 128:(kk + 1) * 128], t_eye_bf[:],
                       wait=([(poo, qdone)] if (kk == 0 and wi < 2) else []) +
                            bk.free_at, inc=(mm, 1))
                    em("scalar", "copy",
                       out=t_asmt(ab)[:, wi * 128:(wi + 1) * 128],
                       in_=bk.apt(BF16),
                       wait=[(mm, mm.n)] + ([(lane, lane.n)]
                                            if wi == 0 else []),
                       inc=(sca, 1), reads=(), writes=(REG_B,))
                    bk.free_at = [(sca, sca.n)]
                em("gpsimd", "dma_start",
                   out=w2qt_s[ch * 2048 + kk * 128:
                              ch * 2048 + (kk + 1) * 128,
                              wg * 512:(wg + 1) * 512],
                   in_=t_asmt(ab)[:, 0:512], wait=[(sca, sca.n)],
                   inc=(lane, 16))
            setm[sset] = [(mm, mm.n)]
    run_block()
    if stop_after == 'B2':
        st.close()
        return nc

    new_block()
    GPB2 = gp_all()
    em("sync", "dma_start",
       out=t_dmaskv().rearrange("p (j f) -> p j f", f=512),
       in_=dmask.rearrange("j p f -> p j f"), wait=gp_all(), inc=(ldc, 16))
    DMLD = ldc.n
    prev_head = []
    bndm = []
    PREPD = []
    VLAST = 0
    for h in range(H):
        em("sync", "dma_start", out=hQRAW(), in_=qraw_s[h, :, :],
           wait=PREPD + (GPB2 if h == 0 else []), inc=(ldh, 16))
        em("sync", "dma_start", out=hKRAW(), in_=kraw_s[h, :, :],
           inc=(ldh, 16))
        em("sync", "dma_start", out=t_bfb[:, 0:N], in_=sig_s[h, :, :],
           inc=(ldh, 16))
        for c in range(TT):
            em("sync", "dma_start",
               out=hVRAW()[:, c * 128:(c + 1) * 128],
               in_=vraw_s[c * 128:(c + 1) * 128, h * 128:(h + 1) * 128],
               wait=([(vec, VLAST)] if c == 0 and h > 0 else []),
               inc=(ldh, 16))
        LDH = ldh.n
        em("scalar", "activation", out=t_w2[:, 0:N], in_=t_bfb[:, 0:N],
           func=ACT.Ln, wait=[(ldh, LDH)], inc=(sca, 1),
           reads=(t_bfb,), writes=(t_w2,))
        em("vector", "tensor_scalar", out=hBETA(), in0=t_w2[:, 0:N],
           scalar1=1.0 / 16.0, scalar2=-16.0, op0=Alu.mult, op1=Alu.max,
           wait=[(sca, sca.n)], inc=(vec, 1), reads=(t_w2,), writes=(REG_B,))
        Ebf = t_w0[:].bitcast(BF16)[:, 0:N]
        SQbf = t_w0[:].bitcast(BF16)[:, N:2 * N]
        for (dst, srcbf, sc_, lo, hi) in [
                (hQSB(), hQRAW(), QS_SCALE, NEPS * QS_SCALE,
                 (1 - NEPS) * QS_SCALE),
                (hKCB(), hKRAW(), 1.0, NEPS, 1 - NEPS)]:
            em("scalar", "activation", out=Ebf, in_=srcbf,
               func=ACT.Exp, wait=[(ldh, LDH), (mm, mm.n), (vec, vec.n)],
               inc=(sca, 1), reads=(REG_B, t_w0), writes=(t_w0,))
            EXD = sca.n
            for j in range(NS):
                bk = banks[4 + j % 2]
                em("tensor", "matmul", bk.ap(512)[0:1, :], t_ones128_bf[:],
                   Ebf[:, j * 512:(j + 1) * 512],
                   wait=[(sca, EXD)] + bk.free_at, start=True, stop=True,
                   inc=(mm, 1))
                em("vector", "tensor_scalar",
                   out=hROWC()[0:1, j * 512:(j + 1) * 512],
                   in0=bk.ap(512)[0:1, :], scalar1=1.0 / sc_, scalar2=None,
                   op0=Alu.mult,
                   wait=[(mm, mm.n)] + (bndm if j == 0 and sc_ != 1.0
                                        else []),
                   inc=(vec, 1), reads=(), writes=(REG_B,))
                bk.free_at = [(vec, vec.n)]
            RD = vec.n
            for j in range(NS):
                bk = banks[6 + j % 2]
                em("tensor", "matmul", bk.ap(512), t_ones_col_bf[:],
                   hROWC()[0:1, j * 512:(j + 1) * 512],
                   wait=[(vec, RD)] + bk.free_at, start=True, stop=True,
                   inc=(mm, 1))
                em("vector", "reciprocal",
                   out=t_w1[:, j * 512:(j + 1) * 512], in_=bk.ap(512),
                   wait=[(mm, mm.n)], reads=(), writes=(t_w1,))
                em("vector", "tensor_tensor",
                   out=t_w1[:, j * 512:(j + 1) * 512],
                   in0=Ebf[:, j * 512:(j + 1) * 512],
                   in1=t_w1[:, j * 512:(j + 1) * 512],
                   op=Alu.mult, reads=(t_w0, t_w1), writes=(t_w1,))
                em("vector", "tensor_scalar",
                   out=dst[:, j * 512:(j + 1) * 512],
                   in0=t_w1[:, j * 512:(j + 1) * 512], scalar1=lo, scalar2=hi,
                   op0=Alu.max, op1=Alu.min, inc=(vec, 1),
                   reads=(t_w1,), writes=(REG_B,))
                bk.free_at = [(vec, vec.n)]
        QKD = vec.n
        em("scalar", "activation", out=SQbf, in_=hKRAW(),
           func=ACT.Square, wait=[(vec, QKD), (mm, mm.n)], inc=(sca, 1),
           reads=(REG_B, t_w0), writes=(t_w0,))
        SQ = sca.n
        for j in range(NS):
            bk = banks[4 + j % 2]
            em("tensor", "matmul", bk.ap(512)[0:1, :], t_ones128_bf[:],
               SQbf[:, j * 512:(j + 1) * 512], wait=[(sca, SQ)] + bk.free_at,
               start=True, stop=True, inc=(mm, 1))
            em("vector", "tensor_copy",
               out=hROW()[0:1, j * 512:(j + 1) * 512],
               in_=bk.ap(512)[0:1, :], wait=[(mm, mm.n)], inc=(vec, 1),
               reads=(), writes=(REG_B,))
            bk.free_at = [(vec, vec.n)]
        em("vector", "reduce_max", out=t_km[0:1, 0:1], in_=hROW()[0:1, 0:N],
           axis=AXX, inc=(vec, 1), reads=(REG_B,), writes=(t_km,))
        KMD = vec.n
        em("scalar", "activation", out=SQbf, in_=hQRAW(),
           func=ACT.Square, wait=[(vec, KMD), (mm, mm.n)], inc=(sca, 1),
           reads=(REG_B, t_w0), writes=(t_w0,))
        SQ = sca.n
        PREPD = [(sca, SQ)]
        for j in range(NS):
            bk = banks[4 + j % 2]
            em("tensor", "matmul", bk.ap(512)[0:1, :], t_ones128_bf[:],
               SQbf[:, j * 512:(j + 1) * 512], wait=[(sca, SQ)] + bk.free_at,
               start=True, stop=True, inc=(mm, 1))
            em("vector", "tensor_copy",
               out=hROWB()[0:1, j * 512:(j + 1) * 512],
               in_=bk.ap(512)[0:1, :], wait=[(mm, mm.n)], inc=(vec, 1),
               reads=(), writes=(REG_B,))
            bk.free_at = [(vec, vec.n)]
        em("vector", "tensor_scalar", out=hROWC()[0:1, 0:N],
           in0=hROWB()[0:1, 0:N], scalar1=t_km[0:1, 0:1], scalar2=-0.5,
           op0=Alu.add, op1=Alu.mult, inc=(vec, 1),
           reads=(REG_B, t_km), writes=(REG_B,))
        BND = vec.n
        em("gpsimd", "dma_start", out=bound_s[h, :, :],
           in_=hROWC()[0:1, 0:N], wait=[(vec, BND)], inc=(gpA, 16))
        bndm = [(gpA, gpA.n)]
        bkS = banks[6]
        bko = banks[7]
        PQm = []
        SAD = 0
        for g in range(NCH // 4):
            cs = [4 * g + i for i in range(4)]
            em("scalar", "copy", out=gQS()[:, 0:512],
               in_=hQSB()[:, 4 * g * 128:(4 * g + 4) * 128],
               wait=[(vec, QKD)] + (PQm if g > 0 else []), inc=(sca, 1),
               reads=(REG_B,), writes=(REG_R,))
            em("scalar", "copy", out=gKC()[:, 0:512],
               in_=hKCB()[:, 4 * g * 128:(4 * g + 4) * 128],
               inc=(sca, 1), reads=(REG_B,), writes=(REG_R,))
            GCV = sca.n
            for i, c in enumerate(cs):
                em("vector", "tensor_tensor",
                   out=dKBG()[:, i * 128:(i + 1) * 128],
                   in0=gKC()[:, i * 128:(i + 1) * 128],
                   in1=hBETA()[:, c * 128:(c + 1) * 128], op=Alu.mult,
                   wait=([(sca, GCV)] if i == 0 else []) +
                        (PQm if i == 0 else []),
                   inc=(vec, 1), reads=(REG_B, REG_R), writes=(REG_R,))
                bk = banks[4 + i % 2]
                em("tensor", "transpose", bk.apt(F32),
                   hBETA()[:, c * 128:(c + 1) * 128], t_eye_f[:],
                   wait=[(vec, vec.n)] + bk.free_at, inc=(mm, 1))
                em("scalar", "copy", out=dBTG()[:, i * 128:(i + 1) * 128],
                   in_=bk.apt(F32), wait=[(mm, mm.n)], inc=(sca, 1),
                   reads=(), writes=(REG_B,))
                bk.free_at = [(sca, sca.n)]
                bk2 = banks[2 + i % 2]
                em("tensor", "transpose", bk2.apt(F32R),
                   gKC()[:, i * 128:(i + 1) * 128], t_eye_r[:],
                   wait=bk2.free_at, inc=(mm, 1))
                em("scalar", "copy", out=dKCTM()[:, i * 128:(i + 1) * 128],
                   in_=bk2.apt(F32R), wait=[(mm, mm.n)], inc=(sca, 1),
                   reads=(), writes=(REG_R,))
                bk2.free_at = [(sca, sca.n)]
                em("vector", "tensor_tensor",
                   out=dPa()[:, i * 256:i * 256 + 128],
                   in0=hVRAW()[:, c * 128:(c + 1) * 128],
                   in1=dBTG()[:, i * 128:(i + 1) * 128], op=Alu.mult,
                   wait=[(sca, sca.n)], inc=(vec, 1),
                   reads=(REG_B, REG_R), writes=(REG_R,))
                em("vector", "tensor_tensor",
                   out=dPa()[:, i * 256 + 128:i * 256 + 256],
                   in0=dKCTM()[:, i * 128:(i + 1) * 128],
                   in1=dBTG()[:, i * 128:(i + 1) * 128], op=Alu.mult,
                   inc=(vec, 1), reads=(REG_B, REG_R), writes=(REG_R,))
            PREP = vec.n
            if g == NCH // 4 - 1:
                VLAST = PREP
            for i, c in enumerate(cs):
                em("tensor", "matmul", banks[4].ap(128, off=i * 128),
                   dKBG()[:, i * 128:(i + 1) * 128],
                   gKC()[:, i * 128:(i + 1) * 128],
                   wait=([(vec, PREP)] + banks[4].free_at) if i == 0 else [],
                   start=True, stop=True, inc=(mm, 1) if i == 3 else None)
                em("tensor", "matmul", banks[5].ap(128, off=i * 128),
                   gKC()[:, i * 128:(i + 1) * 128],
                   dKBG()[:, i * 128:(i + 1) * 128],
                   wait=banks[5].free_at if i == 0 else [],
                   start=True, stop=True, inc=(mm, 1) if i == 3 else None)
            AMM = mm.n
            mlb = t_mlneg[:].unsqueeze(1).broadcast_to([128, 4, 128])
            mub = t_muneg[:].unsqueeze(1).broadcast_to([128, 4, 128])
            em("vector", "tensor_tensor",
               out=dQ(0)[:, 0:512].rearrange("p (g f) -> p g f", f=128),
               in0=banks[4].ap(512).rearrange("p (g f) -> p g f", f=128),
               in1=mlb, op=Alu.mult, wait=[(mm, AMM)],
               inc=(vec, 1), reads=(t_mlneg, REG_R), writes=(REG_R,))
            em("vector", "tensor_tensor",
               out=dQ(1)[:, 0:512].rearrange("p (g f) -> p g f", f=128),
               in0=banks[5].ap(512).rearrange("p (g f) -> p g f", f=128),
               in1=mub, op=Alu.mult,
               inc=(vec, 1), reads=(t_muneg,), writes=(REG_R,))
            banks[4].free_at = [(vec, vec.n)]
            banks[5].free_at = [(vec, vec.n)]
            P, Pn_, Q, QT = dPa(), dPb(), dQ(0), dQ(1)
            for it in range(7):
                last = (it == 6)
                PW = vec.n
                SCD = sca.n
                if not last:
                    for i in range(4):
                        em("tensor", "matmul", banks[2].ap(128, off=i * 128),
                           QT[:, i * 128:(i + 1) * 128],
                           Q[:, i * 128:(i + 1) * 128],
                           wait=(banks[2].free_at + [(vec, PW), (sca, SCD)])
                           if i == 0 else [],
                           start=True, stop=True,
                           inc=(mm, 1) if i == 3 else None)
                        em("tensor", "matmul", banks[3].ap(128, off=i * 128),
                           Q[:, i * 128:(i + 1) * 128],
                           QT[:, i * 128:(i + 1) * 128],
                           wait=(banks[3].free_at + [(vec, PW), (sca, SCD)])
                           if i == 0 else [],
                           start=True, stop=True,
                           inc=(mm, 1) if i == 3 else None)
                for i in range(4):
                    bk = banks[i % 2]
                    em("tensor", "matmul", bk.ap(256, off=(i // 2) * 256),
                       QT[:, i * 128:(i + 1) * 128],
                       P[:, i * 256:(i + 1) * 256],
                       wait=([(vec, PW), (sca, SCD)] + bk.free_at)
                       if i < 2 else [],
                       start=True, stop=True,
                       inc=(mm, 1) if i >= 2 else None)
                PMM = mm.n
                QMM = PMM
                for half in range(2):
                    em("vector", "scalar_tensor_tensor",
                       out=Pn_[:, half * 512:(half + 1) * 512],
                       in0=banks[half].ap(512), scalar=0.0,
                       in1=P[:, half * 512:(half + 1) * 512],
                       op0=Alu.add, op1=Alu.add, wait=[(mm, PMM)],
                       inc=(vec, 1), reads=(REG_R,), writes=(REG_R,))
                    banks[half].free_at = [(vec, vec.n)]
                if not last:
                    em("scalar", "copy", out=Q[:, 0:512],
                       in_=banks[2].ap(512), wait=[(mm, QMM)], inc=(sca, 1),
                       reads=(), writes=(REG_R,))
                    banks[2].free_at = [(sca, sca.n)]
                    em("scalar", "copy", out=QT[:, 0:512],
                       in_=banks[3].ap(512), wait=[(mm, QMM)], inc=(sca, 1),
                       reads=(), writes=(REG_R,))
                    banks[3].free_at = [(sca, sca.n)]
                    P, Pn_ = Pn_, P
            UWB = Pn_
            UWD = vec.n
            for i, c in enumerate(cs):
                if c > 0:
                    bk = banks[2]
                    em("tensor", "transpose", bk.apt(F32R),
                       UWB[:, i * 256 + 128:(i + 1) * 256], t_eye_r[:],
                       wait=([(vec, UWD)] if i == 0 else []) + bk.free_at,
                       inc=(mm, 1))
                    em("scalar", "copy", out=t_wT[:], in_=bk.apt(F32R),
                       wait=[(mm, mm.n)], inc=(sca, 1), reads=(),
                       writes=(t_wT,))
                    bk.free_at = [(sca, sca.n)]
                    bk = banks[3]
                    em("tensor", "matmul", bk.ap(128), t_wT[:], t_S[:],
                       wait=[(sca, sca.n), (vec, SAD)] + bk.free_at,
                       start=True, stop=True, inc=(mm, 1))
                    em("vector", "scalar_tensor_tensor", out=t_up[:],
                       in0=bk.ap(128), scalar=-1.0,
                       in1=UWB[:, i * 256:i * 256 + 128],
                       op0=Alu.mult, op1=Alu.add, wait=[(mm, mm.n)],
                       inc=(vec, 1), reads=(REG_R, t_S), writes=(t_up,))
                    bk.free_at = [(vec, vec.n)]
                    up_ap = t_up[:]
                    UPD = vec.n
                else:
                    up_ap = UWB[:, 0:128]
                    UPD = UWD
                bk = banks[4 + (i % 2)]
                em("tensor", "matmul", bk.ap(128, off=256),
                   gKC()[:, i * 128:(i + 1) * 128],
                   gQS()[:, i * 128:(i + 1) * 128],
                   wait=bk.free_at + [(vec, UPD)],
                   start=True, stop=True, inc=(mm, 1))
                em("vector", "tensor_tensor", out=dKBG()[:, 0:128],
                   in0=bk.ap(128, off=256), in1=t_muincl[:], op=Alu.mult,
                   wait=[(mm, mm.n)], inc=(vec, 1),
                   reads=(t_muincl, REG_R), writes=(REG_R,))
                bk.free_at = [(vec, vec.n)]
                ATD = vec.n
                if c > 0:
                    em("tensor", "matmul", bko.ap(128), t_S[:],
                       gQS()[:, i * 128:(i + 1) * 128],
                       wait=bko.free_at + [(vec, ATD)], start=True,
                       stop=False)
                    em("tensor", "matmul", bko.ap(128), up_ap,
                       dKBG()[:, 0:128], start=False, stop=True, inc=(mm, 1))
                else:
                    em("tensor", "matmul", bko.ap(128), up_ap,
                       dKBG()[:, 0:128],
                       wait=[(vec, ATD)] + bko.free_at, start=True,
                       stop=True, inc=(mm, 1))
                em("vector", "tensor_copy",
                   out=attnpre(h)[:, c * 128:(c + 1) * 128],
                   in_=bko.ap(128),
                   wait=[(mm, mm.n)], inc=(vec, 1), reads=(),
                   writes=(REG_A,))
                bko.free_at = [(vec, vec.n)]
                em("tensor", "matmul", bkS.ap(128),
                   dKCTM()[:, i * 128:(i + 1) * 128], up_ap,
                   wait=[(vec, UPD)] + bkS.free_at, start=True, stop=True,
                   inc=(mm, 1))
                if c < NCH - 1:
                    if c == 0:
                        em("vector", "tensor_copy", out=t_S[:],
                           in_=bkS.ap(128), wait=[(mm, mm.n)], inc=(vec, 1),
                           reads=(), writes=(t_S,))
                    else:
                        em("vector", "tensor_tensor", out=t_S[:], in0=t_S[:],
                           in1=bkS.ap(128), op=Alu.add, wait=[(mm, mm.n)],
                           inc=(vec, 1), reads=(t_S,), writes=(t_S,))
                    bkS.free_at = [(vec, vec.n)]
                    SAD = vec.n
                else:
                    bkS.free_at = [(mm, mm.n)]
            PQm = [(vec, vec.n), (sca, sca.n)]
        prev_head = [(vec, vec.n), (sca, sca.n), (mm, mm.n)]
    o2 = 26624

    def q2(st): return vB(o2 + st * 19456, BF16, N)

    def k2(st): return vB(o2 + st * 19456 + 4096, BF16, N)

    def v2(st): return vB(o2 + st * 19456 + 8192, BF16, N)

    def rowc2(st): return vB(o2 + st * 19456 + 12288, BF16, N)

    def rowd2(st): return vB(o2 + st * 19456 + 16384, BF16, 512)
    p2m = [list(prev_head) + [(gpA, gpA.n)],
           list(prev_head) + [(gpA, gpA.n)]]
    lds2 = [ldh, ld0]
    for h in range(H):
        st_ = h % 2
        lane2 = lds2[st_]
        em("sync", "dma_start", out=q2(st_), in_=qraw_s[h, :, :],
           wait=p2m[st_], inc=(lane2, 16))
        em("sync", "dma_start", out=k2(st_), in_=kraw_s[h, :, :],
           inc=(lane2, 16))
        em("sync", "dma_start", out=rowc2(st_)[0:1, 0:N],
           in_=bound_s[h, :, :], inc=(lane2, 16))
        for c in range(TT):
            em("sync", "dma_start",
               out=v2(st_)[:, c * 128:(c + 1) * 128],
               in_=vraw_s[c * 128:(c + 1) * 128, h * 128:(h + 1) * 128],
               inc=(lane2, 16))
        LDH2 = lane2.n
        for s in range(NS):
            nk = (s + 1) * 4
            bk_sum = banks[4]
            bk_o = banks[5]
            ptm = [[], []]
            pendc = None
            for kc in range(nk + 1):
                if kc < nk:
                    pb = kc % 2
                    bk = banks[pb]
                    PT = t_bfa if pb == 0 else t_bfb
                    em("tensor", "matmul", bk.ap(512),
                       k2(st_)[:, kc * 128:(kc + 1) * 128],
                       q2(st_)[:, s * 512:(s + 1) * 512],
                       wait=bk.free_at + ([(lane2, LDH2)] if kc == 0
                                          and s == 0 else []),
                       start=True, stop=False)
                    em("tensor", "matmul", bk.ap(512), t_ones_col_bf[:],
                       rowc2(st_)[0:1, s * 512:(s + 1) * 512],
                       start=False, stop=True, inc=(mm, 1))
                    em("scalar", "activation", out=PT[:, 0:512],
                       in_=bk.ap(512),
                       func=ACT.Exp, scale=ISQ, wait=[(mm, mm.n)] + ptm[pb],
                       inc=(sca, 1), reads=(), writes=(PT,))
                    bk.free_at = [(sca, sca.n)]
                    if kc >= nk - 4:
                        em("vector", "tensor_tensor", out=PT[:, 0:512],
                           in0=PT[:, 0:512],
                           in1=t_dmaskv()[:, (kc - (nk - 4)) * 512:
                                          (kc - (nk - 4) + 1) * 512],
                           op=Alu.mult,
                           wait=[(sca, sca.n)] + ([(ldc, DMLD)]
                                                  if h == 0 and s == 0 and
                                                  kc == 0 else []),
                           inc=(vec, 1), reads=(PT,), writes=(PT,))
                        PTD = [(vec, vec.n)]
                    else:
                        PTD = [(sca, sca.n)]
                    pend_new = (kc, PT, PTD)
                else:
                    pend_new = None
                if pendc is not None:
                    (kp, PTp, PTDp) = pendc
                    em("tensor", "matmul", bk_sum.ap(512)[0:1, :],
                       t_ones128_bf[:], PTp[:, 0:512],
                       wait=PTDp + (bk_sum.free_at if kp == 0 else []),
                       start=(kp == 0), stop=(kp == nk - 1), inc=(mm, 1))
                    em("tensor", "matmul", bk_o.ap(512),
                       v2(st_)[:, kp * 128:(kp + 1) * 128], PTp[:, 0:512],
                       wait=(bk_o.free_at if kp == 0 else []),
                       start=(kp == 0), stop=(kp == nk - 1), inc=(mm, 1))
                    ptm[kp % 2] = [(mm, mm.n)]
                pendc = pend_new
            SMM = mm.n
            em("vector", "tensor_scalar", out=rowd2(st_)[0:1, 0:512],
               in0=bk_sum.ap(512)[0:1, :], scalar1=2.0, scalar2=None,
               op0=Alu.mult, wait=[(mm, SMM)], inc=(vec, 1),
               reads=(), writes=(REG_B,))
            bk_sum.free_at = [(vec, vec.n)]
            bkb = banks[2]
            em("tensor", "matmul", bkb.ap(512), t_ones_col_bf[:],
               rowd2(st_)[0:1, 0:512], wait=[(vec, vec.n)] + bkb.free_at,
               start=True, stop=True, inc=(mm, 1))
            em("vector", "reciprocal", out=t_w2[:, 1024:1536],
               in_=bkb.ap(512), wait=[(mm, mm.n)], inc=(vec, 1),
               reads=(t_w2,), writes=(t_w2,))
            bkb.free_at = [(vec, vec.n)]
            em("vector", "tensor_tensor", out=t_w2[:, 1536:2048],
               in0=bk_o.ap(512), in1=t_w2[:, 1024:1536], op=Alu.mult,
               reads=(t_w2,), writes=(t_w2,))
            bk_o.free_at = [(vec, vec.n + 1)]
            em("vector", "tensor_tensor",
               out=attnpre(h)[:, s * 512:(s + 1) * 512],
               in0=attnpre(h)[:, s * 512:(s + 1) * 512],
               in1=t_w2[:, 1536:2048],
               op=Alu.add, inc=(vec, 1), reads=(t_w2, REG_A),
               writes=(REG_A,))
        p2m[st_] = [(vec, vec.n), (sca, sca.n), (mm, mm.n)]
    run_block()
    if stop_after == 'B3':
        st.close()
        return nc

    new_block()
    GPB2 = gp_all()
    for srcrow, dstf in [(lnw, lnw_rep), (lnb, lnb_rep)]:
        for j in range(D // 512):
            em("sync", "dma_start", out=t_rows[0:1, 0:512],
               in_=srcrow[0:1, j * 512:(j + 1) * 512],
               wait=[(vec, vec.n), (mm, mm.n)], inc=(ldc, 16))
            bk = banks[6]
            em("tensor", "matmul", bk.ap(512), t_ones_col[:],
               t_rows[0:1, 0:512], wait=[(ldc, ldc.n)] + bk.free_at,
               start=True, stop=True, inc=(mm, 1))
            em("vector", "tensor_copy",
               out=dstf()[:, j * 512:(j + 1) * 512], in_=bk.ap(512),
               wait=[(mm, mm.n)], inc=(vec, 1), reads=(), writes=(REG_B,))
            bk.free_at = [(vec, vec.n)]
            ewait("sync", (vec, vec.n))
    wsm = [[], []]
    xm = []
    pend4 = []
    TR4 = {}

    def b4_flush():
        (t_, QD_) = pend4.pop(0)
        dst_ = t_bfb if t_ % 2 == 0 else t_bfa
        for k in range(KD):
            bk_ = banks[4 + (k % 2)]
            em("tensor", "transpose", bk_.apt(BF16),
               dst_[:, k * 128:(k + 1) * 128], t_eye_bf[:],
               wait=([(poo, QD_)] if k < 2 else []) + bk_.free_at,
               inc=(mm, 1))
            em("scalar", "copy", out=tqT(k)[:, t_ * 128:(t_ + 1) * 128],
               in_=bk_.apt(BF16), wait=[(mm, mm.n)], inc=(sca, 1))
            bk_.free_at = [(sca, sca.n)]
        TR4[t_] = mm.n
    for t in range(TT):
        em("sync", "dma_start", out=t_w0[:, 0:D],
           in_=x_in[t * 128:(t + 1) * 128, :], wait=xm, inc=(ld0, 16))
        XLD = ld0.n
        for k in range(KD):
            b = k % 2
            lw = ld2 if b == 0 else ld3
            em("sync", "dma_start", out=wo_slab(b),
               in_=wobf_s[k * 128:(k + 1) * 128, :],
               wait=wsm[b] + (GPB2 if t == 0 and k == 0 else []),
               inc=(lw, 16))
            WLD = (lw, lw.n)
            for j in range(D // 512):
                bk = banks[j]
                em("tensor", "matmul", bk.ap(512),
                   attnpre(k)[:, t * 128:(t + 1) * 128],
                   wo_slab(b)[:, j * 512:(j + 1) * 512],
                   wait=([WLD] if j == 0 else []) +
                        (bk.free_at if k == 0 else []),
                   start=(k == 0), stop=(k == KD - 1),
                   inc=(mm, 1) if j == D // 512 - 1 else None)
            wsm[b] = [(mm, mm.n)]
        WOMM = mm.n
        if pend4:
            b4_flush()
        for j in range(D // 512):
            bk = banks[j]
            em("vector", "scalar_tensor_tensor",
               out=t_w2[:, j * 512:(j + 1) * 512], in0=bk.ap(512),
               scalar=0.0, in1=t_w0[:, j * 512:(j + 1) * 512],
               op0=Alu.add, op1=Alu.add, wait=[(mm, WOMM), (ld0, XLD)],
               inc=(vec, 1), reads=(t_w0,), writes=(t_w2,))
            bk.free_at = [(vec, vec.n)]
        xm = [(vec, vec.n)]
        layer_norm_apply(t_w2[:, 0:D], h1(t), inc=(sca, 1),
                         extra_reads=(t_w2,))
        layer_norm_apply(h1(t), t_w1[:, 0:D], inc=(sca, 1),
                         extra_reads=(REG_B, t_w1), wait_src=[(sca, sca.n)])
        srcq = t_bfa if t % 2 == 0 else t_bfb
        dstq = t_bfb if t % 2 == 0 else t_bfa
        em("vector", "tensor_tensor", out=t_w1[:, 0:D], in0=t_w1[:, 0:D],
           in1=lnw_rep(), op=Alu.mult, wait=[(sca, sca.n)],
           reads=(t_w1, REG_B), writes=(t_w1,))
        em("vector", "tensor_tensor", out=srcq[:, 0:D], in0=t_w1[:, 0:D],
           in1=lnb_rep(), op=Alu.add, inc=(vec, 1),
           wait=([(mm, TR4[t - 2])] if t - 2 in TR4 else []),
           reads=(t_w1,), writes=(srcq,))
        quant_tile(srcq[:, 0:D], D, dstq[:, 0:D], inc=(poo, 1),
                   wait_src=[(vec, vec.n)], src_reads=(srcq,),
                   dst_wait=([(mm, TR4[t - 1])] if t - 1 in TR4 else []))
        pend4.append((t, poo.n))
    while pend4:
        b4_flush()
    run_block()
    if stop_after == 'B4':
        st.close()
        return nc

    new_block()
    gql = [gpA, gpB]
    gqslot = [[], []]
    w1m = [gp_all(), gp_all()]
    SLDv = [0, 0]
    b1m = []

    def w1sl(sbuf, k):
        if sbuf == 0:
            return w1slab()[:, k * 512:(k + 1) * 512]
        tt_ = t_w0 if k < KD // 2 else t_w1
        kk_ = k % (KD // 2)
        return tt_[:].bitcast(BF16)[:, kk_ * 512:(kk_ + 1) * 512]

    b1c = t_w2[:, 512:1024]
    TRMMv = {}

    def b5_dst(t):
        return t_bfb[:, 512 + (t % 2) * 512:1024 + (t % 2) * 512]

    def b5_flush(pend):
        (t, QD, EVD, slot, bk) = pend.pop(0)
        for i in range(4):
            em("tensor", "transpose", bk.apt(BF16, 128, 64 * i),
               b5_dst(t)[:, i * 128:(i + 1) * 128], t_eye_bf[:],
               wait=(([(poo, QD), (vec, EVD)] + bk.free_at) if i == 0
                     else []), inc=(mm, 1))
        TRMMv[t] = mm.n
        stg = t_bfa[:, 512 + slot * 512:1024 + slot * 512]
        em("scalar", "copy", out=stg, in_=bk.apt(BF16, 512, 0),
           wait=[(mm, mm.n)] + gqslot[slot], inc=(sca, 1),
           reads=(), writes=(t_bfa,))
        bk.free_at = [(sca, sca.n)]
        lane_g = gql[slot]
        fc_, t_ = t // TT, t % TT
        for i in range(4):
            em("gpsimd", "dma_start",
               out=gqT_s[fc_ * 512 + i * 128:fc_ * 512 + (i + 1) * 128,
                         t_ * 128:(t_ + 1) * 128],
               in_=stg[:, i * 128:(i + 1) * 128],
               wait=([(sca, sca.n)] if i == 0 else []),
               inc=(lane_g, 16))
        gqslot[slot] = [(lane_g, lane_g.n)]
    pend5 = []
    for fc in range(FD):
        sb = fc % 2
        lane_s = ld2 if sb == 0 else ld3
        em("sync", "dma_start", out=t_rows[0:1, 0:512],
           in_=b1[0:1, fc * 512:(fc + 1) * 512], wait=b1m, inc=(ldc, 16))
        B1LD = ldc.n
        bk0 = banks[0]
        em("tensor", "matmul", bk0.ap(512), t_ones_col[:],
           t_rows[0:1, 0:512], wait=[(ldc, B1LD)] + bk0.free_at,
           start=True, stop=True, inc=(mm, 1))
        b1m = [(mm, mm.n)]
        em("vector", "tensor_copy", out=b1c, in_=bk0.ap(512),
           wait=[(mm, mm.n)], inc=(vec, 1), reads=(), writes=(t_w2,))
        bk0.free_at = [(vec, vec.n)]
        for k in range(KD):
            em("sync", "dma_start", out=w1sl(sb, k),
               in_=w1qt_s[k * 128:(k + 1) * 128, fc * 512:(fc + 1) * 512],
               wait=(w1m[sb] if k == 0 else []), inc=(lane_s, 16))
        SLDv[sb] = lane_s.n
        for t in range(TT):
            gt = fc * TT + t
            bk = banks[t % 8]
            for k in range(KD):
                em("tensor", "matmul", bk.ap(512),
                   tqT(k)[:, t * 128:(t + 1) * 128], w1sl(sb, k),
                   wait=([(lane_s, SLDv[sb])] if k == 0 else []) +
                        (bk.free_at if k == 0 else []),
                   start=(k == 0), stop=(k == KD - 1),
                   inc=(mm, 1) if k == KD - 1 else None)
            MMD = mm.n
            if len(pend5) >= 2:
                b5_flush(pend5)
            em("vector", "scalar_tensor_tensor", out=t_bfa[:, 0:512],
               in0=bk.ap(512), scalar=0.0, in1=b1c, op0=Alu.add,
               op1=Alu.add, wait=[(mm, MMD)], inc=(vec, 1),
               reads=(t_w2,), writes=(t_bfa,))
            EVD = vec.n
            em("scalar", "activation", out=t_bfb[:, 0:512],
               in_=t_bfa[:, 0:512], func=ACT.Gelu_apprx_tanh,
               wait=[(vec, EVD)], inc=(sca, 1),
               reads=(t_bfa,), writes=(t_bfb,))
            dwt = ([(mm, TRMMv[gt - 2])] if gt - 2 in TRMMv else [])
            quant_tile(t_bfb[:, 0:512], 512, b5_dst(gt),
                       inc=(poo, 1), wait_src=[(sca, sca.n)],
                       src_reads=(t_bfb,), dst_wait=dwt)
            pend5.append((gt, poo.n, EVD, gt % 2, bk))
        while pend5:
            b5_flush(pend5)
        w1m[sb] = [(mm, mm.n)]
    run_block()
    if stop_after == 'B5':
        st.close()
        return nc

    new_block()
    GPB6 = gp_all()
    for j in range(D // 512):
        em("sync", "dma_start", out=t_rows[0:1, 0:512],
           in_=b2[0:1, j * 512:(j + 1) * 512],
           wait=[(vec, vec.n)] + (GPB6 if j == 0 else []), inc=(ldc, 16))
        em("vector", "tensor_copy", out=b2row()[0:1, j * 512:(j + 1) * 512],
           in_=t_rows[0:1, 0:512], wait=[(ldc, ldc.n)], inc=(vec, 1),
           reads=(), writes=(REG_B,))
        ewait("sync", (vec, vec.n))
    B2RD = vec.n
    t_w2b = t_w2[:].bitcast(BF16)

    def gqT8q():
        return vA(0, BF16, KF * 512)

    def stage6(i):
        return [t_bfa, t_bfb][i] if i < 2 else \
            t_w2b[:, (i - 2) * 2048:(i - 1) * 2048]
    lgq = [ldh, ld0]
    lpz = [ld2, ld3, ldc, ld1]
    qm = list(GPB6)
    w2m = [[], [], [], []]

    def w2p(slot):
        return vB(TAIL + slot * 1024, BF16, 512)
    for qd in range(TT // 4):
        for f in range(KF):
            lg = lgq[f % 2]
            em("sync", "dma_start",
               out=gqT8q()[:, f * 512:(f + 1) * 512],
               in_=gqT_s[f * 128:(f + 1) * 128,
                         qd * 512:(qd + 1) * 512],
               wait=((qm + [(ldh, ldh.n), (ld0, ld0.n)]) if f < 2 else []),
               inc=(lg, 16))
        GQD = [(ldh, ldh.n), (ld0, ld0.n)]
        for j in range(D // 512):
            bs = (j % 2) * 4
            for f in range(KF):
                b = f % 4
                lw = lpz[b]
                em("sync", "dma_start", out=w2p(b),
                   in_=w2qt_s[f * 128:(f + 1) * 128,
                              j * 512:(j + 1) * 512],
                   wait=w2m[b], inc=(lw, 16))
                WLD = (lw, lw.n)
                for i in range(4):
                    bk = banks[bs + i]
                    em("tensor", "matmul", bk.ap(512),
                       gqT8q()[:, f * 512 + i * 128:
                               f * 512 + (i + 1) * 128],
                       w2p(b),
                       wait=([WLD] if i == 0 else []) +
                            (GQD if f == 0 and i == 0 and j == 0 else []) +
                            (bk.free_at if f == 0 else []),
                       start=(f == 0), stop=False,
                       inc=(mm, 1) if i == 3 else None)
                w2m[b] = [(mm, mm.n)]
            for i in range(4):
                bk = banks[bs + i]
                em("tensor", "matmul", bk.ap(512), t_ones_col[:],
                   b2row()[0:1, j * 512:(j + 1) * 512],
                   wait=([(vec, B2RD)] if qd == 0 and j == 0 and i == 0
                         else []),
                   start=False, stop=True, inc=(mm, 1) if i == 3 else None)
            YMM = mm.n
            for i in range(4):
                bk = banks[bs + i]
                em("vector", "tensor_copy",
                   out=stage6(i)[:, j * 512:(j + 1) * 512], in_=bk.ap(512),
                   wait=[(mm, YMM)], inc=(vec, 1),
                   reads=(), writes=(t_bfa, t_bfb, t_w2))
                bk.free_at = [(vec, vec.n)]
        for i in range(4):
            t = 4 * qd + i
            em("vector", "tensor_tensor", out=stage6(i)[:, 0:D],
               in0=stage6(i)[:, 0:D], in1=h1(t), op=Alu.add,
               reads=(t_bfa, t_bfb, t_w2, REG_B),
               writes=(t_bfa, t_bfb, t_w2))
            em("vector", "tensor_tensor", out=t_w0[:, 0:D],
               in0=stage6(i)[:, 0:D], in1=h1(t), op=Alu.add, inc=(vec, 1),
               wait=[(sca, sca.n)],
               reads=(t_bfa, t_bfb, t_w2), writes=(t_w0,))
            layer_norm_apply(t_w0[:, 0:D], t_w1[:, 0:D], inc=(sca, 1),
                             extra_reads=(t_w0, t_w1),
                             act_wait=[(out_c, out_c.n)])
            em("sync", "dma_start", out=y_out[t * 128:(t + 1) * 128, :],
               in_=t_w1[:, 0:D], wait=[(sca, sca.n)], inc=(out_c, 16))
        qm = [(mm, mm.n)]
    ewait("sync", (out_c, out_c.n))
    run_block()
    if stop_after == 'B6':
        st.close()
        return nc

    st.close()
    nc._dbg_ctrs = {k: v.n for k, v in [("ldc", ldc), ("ld0", ld0),
        ("ld1", ld1), ("ld2", ld2), ("ld3", ld3), ("ldh", ldh), ("mm", mm),
        ("vec", vec), ("sca", sca), ("gpA", gpA), ("gpB", gpB),
        ("gpC", gpC), ("gpD", gpD), ("out", out_c)]}
    return nc


_NC_CACHE = {}


def kernel(L0_attn_x, L0_attn_Wq, L0_attn_Wk, L0_attn_Wv, L0_attn_Wo,
           L0_ffn_ln_w, L0_ffn_ln_b, L0_ffn_W1, L0_ffn_b1, L0_ffn_W2,
           L0_ffn_b2):
    cfg = Cfg(N=2048, D=2048, H=16, DFF=8192)
    if "full" not in _NC_CACHE:
        _NC_CACHE["full"] = build_program(cfg)
    nc = _NC_CACHE["full"]
    cst = _consts(cfg)
    shared = dict(
        Wq=np.ascontiguousarray(L0_attn_Wq, np.float32),
        Wk=np.ascontiguousarray(L0_attn_Wk, np.float32),
        Wv=np.ascontiguousarray(L0_attn_Wv, np.float32),
        Wo=np.ascontiguousarray(L0_attn_Wo, np.float32),
        lnw=np.ascontiguousarray(L0_ffn_ln_w, np.float32).reshape(1, -1),
        lnb=np.ascontiguousarray(L0_ffn_ln_b, np.float32).reshape(1, -1),
        W1=np.ascontiguousarray(L0_ffn_W1, np.float32),
        b1=np.ascontiguousarray(L0_ffn_b1, np.float32).reshape(1, -1),
        W2=np.ascontiguousarray(L0_ffn_W2, np.float32),
        b2=np.ascontiguousarray(L0_ffn_b2, np.float32).reshape(1, -1),
        **cst,
    )
    x = np.ascontiguousarray(L0_attn_x, np.float32)
    in_maps = [dict(shared, x_in=x[i]) for i in range(8)]
    res = run_bass_kernel_spmd(nc, in_maps, core_ids=list(range(8)))
    out = np.stack([np.asarray(r["y_out"], np.float32) for r in res.results])
    return out


# version 21 (final)
# speedup vs baseline: 1.0129x; 1.0129x over previous
import numpy as np
import ml_dtypes
from contextlib import ExitStack

import concourse.bass as bass
import concourse.mybir as mybir
from concourse.alu_op_type import AluOpType as Alu
from concourse.bass_utils import run_bass_kernel_spmd

F32 = mybir.dt.float32
F32R = mybir.dt.float32r
DBG_B3 = False
BF16 = mybir.dt.bfloat16
I8 = mybir.dt.int8
I32 = mybir.dt.int32
U8 = mybir.dt.uint8
ACT = mybir.ActivationFunctionType
AXX = mybir.AxisListType.X

MAGIC = 3.0 * 2.0 ** 22
NEPS = 1e-6
QS_SCALE = 0.5 * (128.0 ** -0.5)
ISQ = float(1.0 / np.float32(np.sqrt(np.float32(128.0))))


class Cfg:
    def __init__(self, N=2048, D=2048, H=16, DFF=8192):
        self.N, self.D, self.H, self.DFF = N, D, H, DFF
        self.DH = 128
        assert self.H * self.DH == D
        self.KD = D // 128
        self.TT = N // 128
        self.NS = N // 512
        self.NCH = N // 128
        self.KF = DFF // 128
        self.FD = DFF // 512
        self.G8 = N // 256
        assert N % 512 == 0 and DFF % 2048 == 0 and D % 512 == 0
        assert self.NCH % 4 == 0 and self.KD % 4 == 0 and self.KF % 4 == 0


def _consts(cfg):
    c = {}
    c["eye_bf"] = np.eye(128, dtype=ml_dtypes.bfloat16)
    c["eye_f"] = np.eye(128, dtype=np.float32)
    i = np.arange(128)
    c["mask_l_neg"] = np.where(i[:, None] > i[None, :], -1.0, 0.0
                               ).astype(ml_dtypes.bfloat16)
    c["mask_u_neg"] = np.where(i[:, None] < i[None, :], -1.0, 0.0
                               ).astype(ml_dtypes.bfloat16)
    c["mask_u_incl"] = (i[:, None] <= i[None, :]).astype(ml_dtypes.bfloat16)
    f = np.arange(512)
    c["dmask"] = np.stack([((i[:, None] + 128 * j) <= f[None, :])
                           .astype(ml_dtypes.bfloat16) for j in range(4)])
    c["ones_col"] = np.ones((1, 128), np.float32)
    c["ones_col_bf"] = np.ones((1, 128), ml_dtypes.bfloat16)
    c["ones128"] = np.ones((128, 1), np.float32)
    c["ones128_bf"] = np.ones((128, 1), ml_dtypes.bfloat16)
    c["scol"] = np.full((128, 1), 1.0 / QS_SCALE, np.float32)
    return c


def build_program(cfg: Cfg, stop_after=None):
    nc = bass.Bass("TRN2", target_bir_lowering=False)
    N, D, H, DFF = cfg.N, cfg.D, cfg.H, cfg.DFF
    KD, TT, NS, NCH, KF, FD, G8 = (cfg.KD, cfg.TT, cfg.NS, cfg.NCH,
                                   cfg.KF, cfg.FD, cfg.G8)

    def di(name, shape, dt=F32):
        return nc.dram_tensor(name, shape, dt, kind="ExternalInput").ap()

    x_in = di("x_in", [N, D])
    Wq = di("Wq", [D, D]); Wk = di("Wk", [D, D]); Wv = di("Wv", [D, D])
    Wo = di("Wo", [D, D])
    lnw = di("lnw", [1, D]); lnb = di("lnb", [1, D])
    W1 = di("W1", [DFF, D]); b1 = di("b1", [1, DFF])
    W2 = di("W2", [D, DFF]); b2 = di("b2", [1, D])
    eye_bf = di("eye_bf", [128, 128], BF16)
    eye_f = di("eye_f", [128, 128])
    mask_l_neg = di("mask_l_neg", [128, 128], BF16)
    mask_u_neg = di("mask_u_neg", [128, 128], BF16)
    mask_u_incl = di("mask_u_incl", [128, 128], BF16)
    dmask = di("dmask", [4, 128, 512], BF16)
    ones_col = di("ones_col", [1, 128])
    ones_col_bf = di("ones_col_bf", [1, 128], BF16)
    ones128 = di("ones128", [128, 1])
    ones128_bf = di("ones128_bf", [128, 1], BF16)
    scol = di("scol", [128, 1])

    y_out = nc.dram_tensor("y_out", [N, D], F32, kind="ExternalOutput").ap()

    def dsc(name, shape, dt):
        return nc.dram_tensor(name, shape, dt, kind="Internal").ap()

    qraw_s = dsc("qraw_s", [H, 128, N], BF16)
    kraw_s = dsc("kraw_s", [H, 128, N], BF16)
    sig_s = dsc("sig_s", [H, 128, N], BF16)
    vraw_s = dsc("vraw_s", [N, D], BF16)
    wobf_s = dsc("wobf_s", [D, D], BF16)
    bound_s = dsc("bound_s", [H, 1, N], BF16)
    gqT_s = dsc("gqT_s", [DFF, N], BF16)
    w1qt_s = dsc("w1qt_s", [D, DFF], BF16)
    w2qt_s = dsc("w2qt_s", [DFF, D], BF16)
    dbg1 = dsc("dbg1", [128, 512], F32)
    dbg2 = dsc("dbg2", [128, 2048], BF16)

    st = ExitStack()
    A = nc.alloc_sbuf_tensor

    t_eye_bf = A("t_eye_bf", [128, 128], BF16)
    t_eye_f = A("t_eye_f", [128, 128], F32)
    t_eye_r = A("t_eye_r", [128, 128], F32R)
    t_mlneg = A("t_mlneg", [128, 128], BF16)
    t_muneg = A("t_muneg", [128, 128], BF16)
    t_muincl = A("t_muincl", [128, 128], BF16)
    t_ones_col = A("t_ones_col", [1, 128], F32)
    t_ones_col_bf = A("t_ones_col_bf", [1, 128], BF16)
    t_ones128 = A("t_ones128", [128, 1], F32)
    t_ones128_bf = A("t_ones128_bf", [128, 1], BF16)
    t_scol = A("t_scol", [128, 1], F32)

    QB = 64
    t_amax = A("t_amax", [128, QB], F32)
    t_bits = A("t_bits", [128, QB], I32)
    t_ef = A("t_ef", [128, QB], F32)
    t_sh = A("t_sh", [128, QB], I32)
    t_c23 = A("t_c23", [128, QB], I32)
    t_scl = A("t_scl", [128, QB], F32)
    t_inv = A("t_inv", [128, QB], F32)

    t_w0 = A("t_w0", [128, 2048], F32)
    t_w1 = A("t_w1", [128, 2048], F32)
    t_w2 = A("t_w2", [128, 2048], F32)
    t_bfa = A("t_bfa", [128, 2048], BF16)
    t_bfb = A("t_bfb", [128, 2048], BF16)
    t_i8a = A("t_i8a", [128, 2048], I8)
    t_rows = A("t_rows", [1, 512], F32)

    t_wT = A("t_wT", [128, 128], F32R)
    t_up = A("t_up", [128, 128], F32R)
    t_S = A("t_S", [128, 128], F32R)
    t_km = A("t_km", [1, 1], F32)
    t_stats = A("t_stats", [128, 24], F32)
    t_mv = A("t_mv", [128, 2], F32)
    t_sv1 = A("t_sv1", [128, 1], F32)
    t_sv2 = A("t_sv2", [128, 1], F32)
    t_sv3 = A("t_sv3", [128, 1], F32)
    t_sv4 = A("t_sv4", [128, 1], F32)

    REG_A = A("REG_A", [128, 65536], U8)
    REG_B = A("REG_B", [128, 81920], U8)
    REG_R = A("REG_R", [128, 20480], U8)

    def vA(byte_off, dt, cols):
        e = mybir.dt.size(dt)
        assert byte_off % e == 0
        return REG_A[:].bitcast(dt)[:, byte_off // e: byte_off // e + cols]

    def vB(byte_off, dt, cols):
        e = mybir.dt.size(dt)
        assert byte_off % e == 0
        return REG_B[:].bitcast(dt)[:, byte_off // e: byte_off // e + cols]

    def vR(byte_off, cols):
        return REG_R[:].bitcast(F32R)[:, byte_off // 4: byte_off // 4 + cols]

    def xfm(k):
        return vA(2 * k * N, BF16, N)

    def attnpre(k):
        return vA(2 * k * N, BF16, N)

    def tqT(k):
        return vA(2 * k * N, BF16, N)

    def gqT8b(b):
        return vA(b * (KF * 512), BF16, KF * 256)

    def wres(k):
        return vB(2 * k * D, BF16, D)

    def h1(t):
        return vB(2 * t * D, BF16, D)

    def w2qd(j):
        return vB(2 * j * 2048, BF16, 2048)

    BSTG = max(2 * KD * D, 8 * 4096)

    def bstage():
        return vB(BSTG, BF16, 2048)

    def t_asmt(b=0):
        return vB(BSTG + 4096 + b * 1024, BF16, 512)

    o_qraw, o_kraw, o_vraw = 0, 2 * N, 4 * N
    o_beta = 6 * N
    o_btg = 10 * N
    o_dmask = 10 * N + 2048
    o_row = o_dmask + 4096
    o_rowb = o_row + 4 * N
    o_qsb = o_rowb + 4 * N
    o_kcb = o_qsb + 2 * N
    o_rowc = o_kcb + 2 * N
    o_rowd = o_rowc + 2 * N
    assert o_rowd + 1024 <= 81920

    def hQRAW(): return vB(o_qraw, BF16, N)
    def hKRAW(): return vB(o_kraw, BF16, N)
    def hVRAW(): return vB(o_vraw, BF16, N)
    def hBETA(): return vB(o_beta, F32, N)
    def dBTG(): return vB(o_btg, F32, 512)
    def t_dmaskv(): return vB(o_dmask, BF16, 2048)
    def hROW(): return vB(o_row, F32, N)
    def hROWB(): return vB(o_rowb, F32, N)
    def hQSB(): return vB(o_qsb, BF16, N)
    def hKCB(): return vB(o_kcb, BF16, N)
    def hROWC(): return vB(o_rowc, BF16, N)
    def hROWD(): return vB(o_rowd, BF16, 512)

    TAIL = 2 * TT * D

    def lnw_rep(): return vB(TAIL, BF16, D)
    def lnb_rep(): return vB(TAIL + 2 * D, BF16, D)
    def wo_slab(b): return vB(TAIL + 4 * D + 2 * b * D, BF16, D)
    assert TAIL + 8 * D <= 81920

    def w1slab(): return vB(TAIL, BF16, KD * 512)
    assert TAIL + 2 * KD * 512 <= 81920

    def w2slab(b): return vB(TAIL + 2 * b * D, BF16, D)

    def b2row(): return vB(TAIL + 4096, F32, D)

    r_gqs, r_gkc = 0, 2048
    r_Pa, r_Pb = 4096, 8192
    r_Q = 12288
    r_kbg = r_Q + 4096
    r_kctm = r_kbg + 2048
    assert r_kctm + 2048 <= 20480

    def gQS(): return vR(r_gqs, 512)
    def gKC(): return vR(r_gkc, 512)
    def dPa(): return vR(r_Pa, 1024)
    def dPb(): return vR(r_Pb, 1024)
    def dQ(i): return vR(r_Q + 2048 * i, 512)
    def dKBG(): return vR(r_kbg, 512)
    def dKCTM(): return vR(r_kctm, 512)

    PSn = [nc.alloc_psum_tensor(f"ps{i}", [128, 512], F32) for i in range(8)]

    sems = {n: st.enter_context(nc.semaphore(n))
            for n in ["ldc", "ld0", "ld1", "ld2", "ld3", "ldh",
                      "mm", "vec", "sca", "gpA", "gpB", "gpC", "gpD", "out",
                      "poo"]}

    class Ctr:
        __slots__ = ("h", "n")

        def __init__(self, name):
            self.h = sems[name]; self.n = 0

        def inc(self, k=1):
            self.n += k
            return self.n
    ldc = Ctr("ldc"); ld0 = Ctr("ld0"); ld1 = Ctr("ld1")
    ld2 = Ctr("ld2"); ld3 = Ctr("ld3"); ldh = Ctr("ldh")
    mm = Ctr("mm"); vec = Ctr("vec"); sca = Ctr("sca")
    gpA = Ctr("gpA"); gpB = Ctr("gpB"); gpC = Ctr("gpC"); gpD = Ctr("gpD")
    out_c = Ctr("out")
    poo = Ctr("poo")
    ldpp = [ld0, ld1]
    gpl = [gpA, gpB]
    gsl = [gpC, gpD]

    def gp_all():
        return [(gpA, gpA.n), (gpB, gpB.n), (gpC, gpC.n), (gpD, gpD.n)]

    class Bank:
        def __init__(self, idx):
            self.t = PSn[idx]
            self.free_at = []

        def ap(self, cols=512, off=0):
            return self.t[:, off:off + cols]

        def apt(self, dt, cols=128, off_f32=0):
            e = 4 // mybir.dt.size(dt)
            return self.t[:].bitcast(dt)[:, off_f32 * e: off_f32 * e + cols]
    banks = [Bank(i) for i in range(8)]

    ops = None
    ENG = {"sync": lambda: nc.sync, "vector": lambda: nc.vector,
           "scalar": lambda: nc.scalar, "tensor": lambda: nc.tensor,
           "gpsimd": lambda: nc.gpsimd}
    dirty = {e: set() for e in ENG}

    def _nm(x):
        return x.name if hasattr(x, "name") else str(x)

    def em(eng, method, *args, wait=(), inc=None, reads=(), writes=(), **kw):
        need_drain = any(_nm(r) in dirty[eng]
                         for r in tuple(reads) + tuple(writes))
        if need_drain:
            dirty[eng].clear()
        for w_ in writes:
            dirty[eng].add(_nm(w_))
        waits = [((c.h if isinstance(c, Ctr) else c), v) for (c, v) in wait]
        ih = None
        if inc is not None:
            ctr, k = inc
            ih = (ctr.h, k)
            ctr.inc(k)

        def f():
            e = ENG[eng]()
            if need_drain:
                e.drain()
            for (h_, v_) in waits:
                e.wait_ge(h_, v_)
            ins = getattr(e, method)(*args, **kw)
            if ih is not None:
                ins.then_inc(ih[0], ih[1])
        ops[eng].append(f)

    def ewait(eng, *pairs):
        prs = [((c.h if isinstance(c, Ctr) else c), v) for (c, v) in pairs]

        def f():
            e = ENG[eng]()
            for h_, v_ in prs:
                e.wait_ge(h_, v_)
        ops[eng].append(f)

    def run_block():
        nonlocal ops
        myops = ops

        with nc.Block() as blk:
            @blk.sync
            def _(e):
                for f in myops["sync"]:
                    f()

            @blk.vector
            def _(e):
                for f in myops["vector"]:
                    f()

            @blk.scalar
            def _(e):
                for f in myops["scalar"]:
                    f()

            @blk.tensor
            def _(e):
                for f in myops["tensor"]:
                    f()

            @blk.gpsimd
            def _(e):
                for f in myops["gpsimd"]:
                    f()
        ops = None
        for e in ENG:
            dirty[e].clear()
        for b in banks:
            b.free_at = []

    def new_block():
        nonlocal ops
        ops = {e: [] for e in ENG}
        for e in ENG:
            dirty[e].clear()

    def hnd(ap_or_t):
        return ap_or_t.tensor if isinstance(ap_or_t, bass.AP) else ap_or_t

    def quant_tile(src_ap, n_cols, out_bf_ap, *, inc=None, wait_src=(),
                   src_reads=(), dst_wait=()):
        nb = n_cols // 32
        src3 = src_ap.rearrange("p (b k) -> p b k", k=32)
        sreads = tuple(hnd(r) for r in src_reads)
        em("vector", "tensor_reduce", out=t_amax[:, 0:nb], in_=src3, axis=AXX,
           op=Alu.max, apply_absolute_value=True, wait=wait_src,
           reads=sreads, writes=(t_amax,))
        em("vector", "tensor_scalar", out=t_amax[:, 0:nb], in0=t_amax[:, 0:nb],
           scalar1=2.0 ** -30, scalar2=None, op0=Alu.max,
           reads=(t_amax,), writes=(t_amax,))
        em("vector", "tensor_tensor", out=t_bits[:, 0:nb],
           in0=t_amax[:, 0:nb].bitcast(I32), in1=t_c23[:, 0:nb],
           op=Alu.logical_shift_right, reads=(t_amax,), writes=(t_bits,))
        em("vector", "tensor_copy", out=t_ef[:, 0:nb], in_=t_bits[:, 0:nb],
           reads=(t_bits,), writes=(t_ef,))
        em("vector", "tensor_scalar", out=t_sh[:, 0:nb], in0=t_ef[:, 0:nb],
           scalar1=-6.0, scalar2=None, op0=Alu.add, reads=(t_ef,),
           writes=(t_sh,))
        em("vector", "tensor_tensor", out=t_sh[:, 0:nb], in0=t_sh[:, 0:nb],
           in1=t_c23[:, 0:nb], op=Alu.logical_shift_left,
           reads=(t_sh,), writes=(t_sh,))
        em("vector", "tensor_copy", out=t_scl[:, 0:nb],
           in_=t_sh[:, 0:nb].bitcast(F32), reads=(t_sh,), writes=(t_scl,))
        em("vector", "tensor_scalar", out=t_sh[:, 0:nb], in0=t_ef[:, 0:nb],
           scalar1=-1.0, scalar2=260.0, op0=Alu.mult, op1=Alu.add,
           reads=(t_ef, t_scl), writes=(t_sh,))
        em("vector", "tensor_tensor", out=t_sh[:, 0:nb], in0=t_sh[:, 0:nb],
           in1=t_c23[:, 0:nb], op=Alu.logical_shift_left,
           reads=(t_sh,), writes=(t_sh,))
        em("vector", "tensor_copy", out=t_inv[:, 0:nb],
           in_=t_sh[:, 0:nb].bitcast(F32), reads=(t_sh,), writes=(t_inv,))
        invb = t_inv[:, 0:nb].unsqueeze(2).broadcast_to([128, nb, 32])
        sclb = t_scl[:, 0:nb].unsqueeze(2).broadcast_to([128, nb, 32])
        em("vector", "tensor_tensor",
           out=t_w2[:, 0:n_cols].rearrange("p (b k) -> p b k", k=32),
           in0=src3, in1=invb, op=Alu.mult,
           reads=(t_inv,) + sreads, writes=(t_w2,))
        em("vector", "tensor_scalar", out=t_w2[:, 0:n_cols],
           in0=t_w2[:, 0:n_cols], scalar1=MAGIC, scalar2=MAGIC,
           op0=Alu.add, op1=Alu.subtract, reads=(t_w2,), writes=(t_w2,))
        em("vector", "scalar_tensor_tensor",
           out=out_bf_ap.rearrange("p (b k) -> p b k", k=32),
           in0=t_w2[:, 0:n_cols].rearrange("p (b k) -> p b k", k=32),
           scalar=127.0, in1=sclb, op0=Alu.min, op1=Alu.mult,
           wait=list(dst_wait),
           inc=inc, reads=(t_w2, t_scl), writes=(hnd(out_bf_ap),))

    def rstd_chain(var_ap, out_ap):
        em("vector", "tensor_scalar", out=t_sv1[:], in0=var_ap, scalar1=1e-5,
           scalar2=None, op0=Alu.add, reads=(hnd(var_ap),), writes=(t_sv1,),
           inc=(vec, 1))
        em("scalar", "activation", out=t_sv2[:], in_=t_sv1[:], func=ACT.Ln,
           wait=[(vec, vec.n)], reads=(t_sv1,), writes=(t_sv2,))
        em("scalar", "activation", out=out_ap, in_=t_sv2[:], func=ACT.Exp,
           scale=-0.5, reads=(t_sv2,), writes=(hnd(out_ap),), inc=(sca, 1))
        em("vector", "tensor_tensor", out=t_sv3[:], in0=out_ap, in1=out_ap,
           op=Alu.mult, wait=[(sca, sca.n)], reads=(hnd(out_ap), t_sv1),
           writes=(t_sv3,))
        em("vector", "tensor_tensor", out=t_sv3[:], in0=t_sv3[:], in1=t_sv1[:],
           op=Alu.mult, reads=(t_sv3,), writes=(t_sv3,))
        em("vector", "tensor_scalar", out=t_sv3[:], in0=t_sv3[:], scalar1=-0.5,
           scalar2=1.5, op0=Alu.mult, op1=Alu.add, reads=(t_sv3,),
           writes=(t_sv3,))
        em("vector", "tensor_tensor", out=out_ap, in0=out_ap, in1=t_sv3[:],
           op=Alu.mult, reads=(t_sv3, hnd(out_ap)), writes=(hnd(out_ap),))

    def layer_norm_apply(src_ap, dst_ap, inc=None, extra_reads=(),
                         wait_src=(), act_wait=()):
        npc = D // 512
        for j in range(npc):
            em("vector", "bn_stats", out=t_stats[:, 6 * j:6 * j + 6],
               in_=src_ap[:, 512 * j:512 * (j + 1)],
               wait=(wait_src if j == 0 else ()),
               reads=(hnd(src_ap),), writes=(t_stats,))
        em("vector", "bn_aggr", out=t_mv[:], in_=t_stats[:, 0:6 * npc],
           reads=(t_stats,), writes=(t_mv,))
        rstd_chain(t_mv[:, 1:2], t_sv4[:])
        em("vector", "tensor_tensor", out=t_sv1[:], in0=t_mv[:, 0:1],
           in1=t_sv4[:], op=Alu.mult, reads=(t_sv4, t_mv), writes=(t_sv1,))
        em("vector", "tensor_scalar", out=t_sv1[:], in0=t_sv1[:], scalar1=-1.0,
           scalar2=None, op0=Alu.mult, reads=(t_sv1,), writes=(t_sv1,),
           inc=(vec, 1))
        em("scalar", "activation", out=dst_ap, in_=src_ap, func=ACT.Identity,
           scale=t_sv4[:], bias=t_sv1[:], wait=[(vec, vec.n)] + list(act_wait),
           reads=(t_sv4, t_sv1) + tuple(hnd(r) for r in extra_reads),
           writes=(hnd(dst_ap),), inc=inc)

    new_block()
    for dst, src in [(t_eye_bf[:], eye_bf), (t_eye_f[:], eye_f),
                     (t_mlneg[:], mask_l_neg), (t_muneg[:], mask_u_neg),
                     (t_muincl[:], mask_u_incl),
                     (t_ones_col[:], ones_col),
                     (t_ones_col_bf[:], ones_col_bf),
                     (t_ones128[:], ones128),
                     (t_ones128_bf[:], ones128_bf), (t_scol[:], scol)]:
        em("sync", "dma_start", out=dst, in_=src, inc=(ldc, 16))
    em("vector", "memset", t_c23[:], 23)
    em("scalar", "copy", out=t_eye_r[:], in_=t_eye_f[:],
       wait=[(ldc, ldc.n)], inc=(sca, 1))
    bufm = [[(sca, sca.n)], []]
    bfm = [[], []]
    for t in range(TT):
        b = t % 2
        buf = t_w0 if b == 0 else t_w1
        bfb = t_bfa if b == 0 else t_bfb
        em("sync", "dma_start", out=buf[:, 0:D],
           in_=x_in[t * 128:(t + 1) * 128, :], wait=bufm[b],
           inc=(ldpp[b], 16))
        em("vector", "tensor_copy", out=bfb[:, 0:D], in_=buf[:, 0:D],
           wait=[(ldpp[b], ldpp[b].n)] + bfm[b], inc=(vec, 1),
           reads=(buf,), writes=(bfb,))
        bufm[b] = [(vec, vec.n)]
        vc = vec.n
        for k in range(KD):
            bk = banks[4 + (k % 2)]
            em("tensor", "transpose", bk.apt(BF16),
               bfb[:, k * 128:(k + 1) * 128], t_eye_bf[:],
               wait=[(vec, vc)] + ([(ldc, ldc.n)] if t == 0 and k == 0
                                   else []) + bk.free_at, inc=(mm, 1))
            em("scalar", "copy", out=xfm(k)[:, t * 128:(t + 1) * 128],
               in_=bk.apt(BF16), wait=[(mm, mm.n)], inc=(sca, 1))
            bk.free_at = [(sca, sca.n)]
        bfm[b] = [(mm, mm.n)]
    run_block()
    if stop_after == 'B0':
        st.close()
        return nc

    new_block()
    wbm = [[], []]

    def load_w_to_wres(Wsrc):
        for k in range(KD):
            b = k % 2
            buf = t_w0 if b == 0 else t_w1
            em("sync", "dma_start", out=buf[:, 0:D],
               in_=Wsrc[k * 128:(k + 1) * 128, :], wait=wbm[b],
               inc=(ldpp[b], 16))
            em("vector", "tensor_copy", out=wres(k), in_=buf[:, 0:D],
               wait=[(ldpp[b], ldpp[b].n)], inc=(vec, 1),
               reads=(buf,), writes=(REG_B,))
            wbm[b] = [(vec, vec.n)]
        return vec.n

    _b1list = [(Wq, qraw_s, False)]
    if stop_after != "B1q":
        _b1list.append((Wk, kraw_s, True))
    for (Wsrc, dst_s, with_sig) in _b1list:
        wdone = load_w_to_wres(Wsrc)
        for h in range(H):
            for s in range(NS):
                seq = h * NS + s
                pb = seq % 2
                bk = banks[pb]
                for k in range(KD):
                    em("tensor", "matmul", bk.ap(512),
                       wres(k)[:, h * 128:(h + 1) * 128],
                       xfm(k)[:, s * 512:(s + 1) * 512],
                       wait=(([(vec, wdone)] if (h == 0 and s == 0 and k == 0)
                              else []) + (bk.free_at if k == 0 else [])),
                       start=(k == 0), stop=(k == KD - 1),
                       inc=(mm, 1) if k == KD - 1 else None)
                ev = bstage()[:, pb * 512:pb * 512 + 512]
                em("vector", "tensor_copy", out=ev, in_=bk.ap(512),
                   wait=[(mm, mm.n), (gpl[pb], gpl[pb].n)], inc=(vec, 1),
                   reads=(), writes=(REG_B,))
                em("gpsimd", "dma_start",
                   out=dst_s[h, :, s * 512:(s + 1) * 512], in_=ev,
                   wait=[(vec, vec.n)], inc=(gpl[pb], 16))
                if with_sig:
                    sg = t_bfb[:, pb * 512:pb * 512 + 512]
                    em("scalar", "activation", out=sg, in_=bk.ap(512),
                       func=ACT.Sigmoid,
                       wait=[(vec, vec.n), (gsl[pb], gsl[pb].n)],
                       inc=(sca, 1), reads=(), writes=(t_bfb,))
                    em("gpsimd", "dma_start",
                       out=sig_s[h, :, s * 512:(s + 1) * 512], in_=sg,
                       wait=[(sca, sca.n)], inc=(gsl[pb], 16))
                    bk.free_at = [(vec, vec.n), (sca, sca.n)]
                else:
                    bk.free_at = [(vec, vec.n)]
    if stop_after in ("B1q", "B1k"):
        run_block()
        st.close()
        return nc
    wdone = load_w_to_wres(Wv)
    ND4 = D // 512
    for t in range(TT):
        bks = [banks[4 + j] for j in range(ND4)]
        for k in range(KD):
            for j in range(ND4):
                em("tensor", "matmul", bks[j].ap(512),
                   xfm(k)[:, t * 128:(t + 1) * 128],
                   wres(k)[:, j * 512:(j + 1) * 512],
                   wait=(([(vec, wdone)] if (t == 0 and k == 0 and j == 0)
                          else []) + (bks[j].free_at if k == 0 else [])),
                   start=(k == 0), stop=(k == KD - 1),
                   inc=(mm, 1) if (k == KD - 1 and j == ND4 - 1) else None)
        for j in range(ND4):
            em("vector", "tensor_copy",
               out=bstage()[:, j * 512:(j + 1) * 512], in_=bks[j].ap(512),
               wait=[(mm, mm.n)] +
                    ([(gpA, gpA.n)] if j == 0 else []) +
                    ([(gpB, gpB.n), (gpC, gpC.n), (gpD, gpD.n)]
                     if t == 0 and j == 0 else []),
               inc=(vec, 1) if j == ND4 - 1 else None,
               reads=(), writes=(REG_B,))
        for j in range(ND4):
            bks[j].free_at = [(vec, vec.n)]
        em("gpsimd", "dma_start", out=vraw_s[t * 128:(t + 1) * 128, :],
           in_=bstage()[:, 0:D], wait=[(vec, vec.n)], inc=(gpA, 16))
    run_block()
    if stop_after == 'B1':
        st.close()
        return nc

    new_block()
    wbm = [[], []]
    for k in range(KD):
        b = k % 2
        buf = t_w0 if b == 0 else t_w1
        em("sync", "dma_start", out=buf[:, 0:D],
           in_=Wo[k * 128:(k + 1) * 128, :], wait=wbm[b], inc=(ldpp[b], 16))
        dstc = bstage()[:, 0:D] if b == 0 else t_bfa[:, 0:D]
        em("vector", "tensor_copy", out=dstc, in_=buf[:, 0:D],
           wait=[(ldpp[b], ldpp[b].n), (gpl[b], gpl[b].n)], inc=(vec, 1),
           reads=(buf,), writes=(REG_B if b == 0 else t_bfa,))
        wbm[b] = [(vec, vec.n)]
        em("gpsimd", "dma_start", out=wobf_s[k * 128:(k + 1) * 128, :],
           in_=dstc, wait=[(vec, vec.n)], inc=(gpl[b], 16))
    setm = [[], []]
    for wg in range(KF // 4):
        sset = wg % 2
        qd = []
        for wi in range(4):
            wt = wg * 4 + wi
            b = wi % 2
            buf = t_w0 if b == 0 else t_w1
            em("sync", "dma_start", out=buf[:, 0:D],
               in_=W1[wt * 128:(wt + 1) * 128, :], wait=wbm[b],
               inc=(ldpp[b], 16))
            dq = w2qd(4 * sset + wi)[:, 0:D]
            quant_tile(buf[:, 0:D], D, dq, inc=(poo, 1),
                       wait_src=[(ldpp[b], ldpp[b].n)] +
                                (setm[sset] if wi == 0 else []) +
                                (gp_all() if wg == 0 and wi == 0 else []),
                       src_reads=(buf,))
            wbm[b] = [(poo, poo.n)]
            qd.append(dq)
        qdone = poo.n
        for k in range(KD):
            ab = k % 2
            lane = gsl[ab]
            for wi in range(4):
                bk = banks[4 + wi % 2]
                em("tensor", "transpose", bk.apt(BF16),
                   qd[wi][:, k * 128:(k + 1) * 128], t_eye_bf[:],
                   wait=([(poo, qdone)] if (k == 0 and wi < 2) else []) +
                        bk.free_at, inc=(mm, 1))
                em("scalar", "copy",
                   out=t_asmt(ab)[:, wi * 128:(wi + 1) * 128],
                   in_=bk.apt(BF16),
                   wait=[(mm, mm.n)] + ([(lane, lane.n)]
                                        if wi == 0 else []),
                   inc=(sca, 1), reads=(), writes=(REG_B,))
                bk.free_at = [(sca, sca.n)]
            em("gpsimd", "dma_start",
               out=w1qt_s[k * 128:(k + 1) * 128, wg * 512:(wg + 1) * 512],
               in_=t_asmt(ab)[:, 0:512], wait=[(sca, sca.n)], inc=(lane, 16))
        setm[sset] = [(mm, mm.n)]
    NW2C = DFF // 2048
    wgi = 1
    for wg in range(KD // 4):
        for ch in range(NW2C):
            sset = wgi % 2
            wgi += 1
            qd = []
            for wi in range(4):
                wt = wg * 4 + wi
                b = wi % 2
                buf = t_w0 if b == 0 else t_w1
                em("sync", "dma_start", out=buf[:, 0:2048],
                   in_=W2[wt * 128:(wt + 1) * 128,
                          ch * 2048:(ch + 1) * 2048],
                   wait=wbm[b], inc=(ldpp[b], 16))
                dq = w2qd(4 * sset + wi)[:, 0:2048]
                quant_tile(buf[:, 0:2048], 2048, dq, inc=(poo, 1),
                           wait_src=[(ldpp[b], ldpp[b].n)] +
                                    (setm[sset] if wi == 0 else []),
                           src_reads=(buf,))
                wbm[b] = [(poo, poo.n)]
                qd.append(dq)
            qdone = poo.n
            for kk in range(16):
                ab = kk % 2
                lane = gsl[ab]
                for wi in range(4):
                    bk = banks[4 + wi % 2]
                    em("tensor", "transpose", bk.apt(BF16),
                       qd[wi][:, kk * 128:(kk + 1) * 128], t_eye_bf[:],
                       wait=([(poo, qdone)] if (kk == 0 and wi < 2) else []) +
                            bk.free_at, inc=(mm, 1))
                    em("scalar", "copy",
                       out=t_asmt(ab)[:, wi * 128:(wi + 1) * 128],
                       in_=bk.apt(BF16),
                       wait=[(mm, mm.n)] + ([(lane, lane.n)]
                                            if wi == 0 else []),
                       inc=(sca, 1), reads=(), writes=(REG_B,))
                    bk.free_at = [(sca, sca.n)]
                em("gpsimd", "dma_start",
                   out=w2qt_s[ch * 2048 + kk * 128:
                              ch * 2048 + (kk + 1) * 128,
                              wg * 512:(wg + 1) * 512],
                   in_=t_asmt(ab)[:, 0:512], wait=[(sca, sca.n)],
                   inc=(lane, 16))
            setm[sset] = [(mm, mm.n)]
    run_block()
    if stop_after == 'B2':
        st.close()
        return nc

    new_block()
    GPB2 = gp_all()
    em("sync", "dma_start",
       out=t_dmaskv().rearrange("p (j f) -> p j f", f=512),
       in_=dmask.rearrange("j p f -> p j f"), wait=gp_all(), inc=(ldc, 16))
    DMLD = ldc.n
    prev_head = []
    bndm = []
    PREPD = []
    VLAST = 0
    for h in range(H):
        em("sync", "dma_start", out=hQRAW(), in_=qraw_s[h, :, :],
           wait=PREPD + (GPB2 if h == 0 else []), inc=(ldh, 16))
        em("sync", "dma_start", out=hKRAW(), in_=kraw_s[h, :, :],
           inc=(ldh, 16))
        em("sync", "dma_start", out=t_bfb[:, 0:N], in_=sig_s[h, :, :],
           inc=(ldh, 16))
        for c in range(TT):
            em("sync", "dma_start",
               out=hVRAW()[:, c * 128:(c + 1) * 128],
               in_=vraw_s[c * 128:(c + 1) * 128, h * 128:(h + 1) * 128],
               wait=([(vec, VLAST)] if c == 0 and h > 0 else []),
               inc=(ldh, 16))
        LDH = ldh.n
        em("scalar", "activation", out=t_w2[:, 0:N], in_=t_bfb[:, 0:N],
           func=ACT.Ln, wait=[(ldh, LDH)], inc=(sca, 1),
           reads=(t_bfb,), writes=(t_w2,))
        em("vector", "tensor_scalar", out=hBETA(), in0=t_w2[:, 0:N],
           scalar1=1.0 / 16.0, scalar2=-16.0, op0=Alu.mult, op1=Alu.max,
           wait=[(sca, sca.n)], inc=(vec, 1), reads=(t_w2,), writes=(REG_B,))
        Ebf = t_w0[:].bitcast(BF16)[:, 0:N]
        SQbf = t_w0[:].bitcast(BF16)[:, N:2 * N]
        for (dst, srcbf, sc_, lo, hi) in [
                (hQSB(), hQRAW(), QS_SCALE, NEPS * QS_SCALE,
                 (1 - NEPS) * QS_SCALE),
                (hKCB(), hKRAW(), 1.0, NEPS, 1 - NEPS)]:
            em("scalar", "activation", out=Ebf, in_=srcbf,
               func=ACT.Exp, wait=[(ldh, LDH), (mm, mm.n), (vec, vec.n)],
               inc=(sca, 1), reads=(REG_B, t_w0), writes=(t_w0,))
            EXD = sca.n
            for j in range(NS):
                bk = banks[4 + j % 2]
                em("tensor", "matmul", bk.ap(512)[0:1, :], t_ones128_bf[:],
                   Ebf[:, j * 512:(j + 1) * 512],
                   wait=[(sca, EXD)] + bk.free_at, start=True, stop=True,
                   inc=(mm, 1))
                em("vector", "tensor_scalar",
                   out=hROWC()[0:1, j * 512:(j + 1) * 512],
                   in0=bk.ap(512)[0:1, :], scalar1=1.0 / sc_, scalar2=None,
                   op0=Alu.mult,
                   wait=[(mm, mm.n)] + (bndm if j == 0 and sc_ != 1.0
                                        else []),
                   inc=(vec, 1), reads=(), writes=(REG_B,))
                bk.free_at = [(vec, vec.n)]
            RD = vec.n
            for j in range(NS):
                bk = banks[6 + j % 2]
                em("tensor", "matmul", bk.ap(512), t_ones_col_bf[:],
                   hROWC()[0:1, j * 512:(j + 1) * 512],
                   wait=[(vec, RD)] + bk.free_at, start=True, stop=True,
                   inc=(mm, 1))
                em("vector", "reciprocal",
                   out=t_w1[:, j * 512:(j + 1) * 512], in_=bk.ap(512),
                   wait=[(mm, mm.n)], reads=(), writes=(t_w1,))
                em("vector", "tensor_tensor",
                   out=t_w1[:, j * 512:(j + 1) * 512],
                   in0=Ebf[:, j * 512:(j + 1) * 512],
                   in1=t_w1[:, j * 512:(j + 1) * 512],
                   op=Alu.mult, reads=(t_w0, t_w1), writes=(t_w1,))
                em("vector", "tensor_scalar",
                   out=dst[:, j * 512:(j + 1) * 512],
                   in0=t_w1[:, j * 512:(j + 1) * 512], scalar1=lo, scalar2=hi,
                   op0=Alu.max, op1=Alu.min, inc=(vec, 1),
                   reads=(t_w1,), writes=(REG_B,))
                bk.free_at = [(vec, vec.n)]
        QKD = vec.n
        em("scalar", "activation", out=SQbf, in_=hKRAW(),
           func=ACT.Square, wait=[(vec, QKD), (mm, mm.n)], inc=(sca, 1),
           reads=(REG_B, t_w0), writes=(t_w0,))
        SQ = sca.n
        for j in range(NS):
            bk = banks[4 + j % 2]
            em("tensor", "matmul", bk.ap(512)[0:1, :], t_ones128_bf[:],
               SQbf[:, j * 512:(j + 1) * 512], wait=[(sca, SQ)] + bk.free_at,
               start=True, stop=True, inc=(mm, 1))
            em("vector", "tensor_copy",
               out=hROW()[0:1, j * 512:(j + 1) * 512],
               in_=bk.ap(512)[0:1, :], wait=[(mm, mm.n)], inc=(vec, 1),
               reads=(), writes=(REG_B,))
            bk.free_at = [(vec, vec.n)]
        em("vector", "reduce_max", out=t_km[0:1, 0:1], in_=hROW()[0:1, 0:N],
           axis=AXX, inc=(vec, 1), reads=(REG_B,), writes=(t_km,))
        KMD = vec.n
        em("scalar", "activation", out=SQbf, in_=hQRAW(),
           func=ACT.Square, wait=[(vec, KMD), (mm, mm.n)], inc=(sca, 1),
           reads=(REG_B, t_w0), writes=(t_w0,))
        SQ = sca.n
        PREPD = [(sca, SQ)]
        for j in range(NS):
            bk = banks[4 + j % 2]
            em("tensor", "matmul", bk.ap(512)[0:1, :], t_ones128_bf[:],
               SQbf[:, j * 512:(j + 1) * 512], wait=[(sca, SQ)] + bk.free_at,
               start=True, stop=True, inc=(mm, 1))
            em("vector", "tensor_copy",
               out=hROWB()[0:1, j * 512:(j + 1) * 512],
               in_=bk.ap(512)[0:1, :], wait=[(mm, mm.n)], inc=(vec, 1),
               reads=(), writes=(REG_B,))
            bk.free_at = [(vec, vec.n)]
        em("vector", "tensor_scalar", out=hROWC()[0:1, 0:N],
           in0=hROWB()[0:1, 0:N], scalar1=t_km[0:1, 0:1], scalar2=-0.5,
           op0=Alu.add, op1=Alu.mult, inc=(vec, 1),
           reads=(REG_B, t_km), writes=(REG_B,))
        BND = vec.n
        em("gpsimd", "dma_start", out=bound_s[h, :, :],
           in_=hROWC()[0:1, 0:N], wait=[(vec, BND)], inc=(gpA, 16))
        bndm = [(gpA, gpA.n)]
        bkS = banks[6]
        bko = banks[7]
        PQm = []
        SAD = 0
        for g in range(NCH // 4):
            cs = [4 * g + i for i in range(4)]
            em("scalar", "copy", out=gQS()[:, 0:512],
               in_=hQSB()[:, 4 * g * 128:(4 * g + 4) * 128],
               wait=[(vec, QKD)] + (PQm if g > 0 else []), inc=(sca, 1),
               reads=(REG_B,), writes=(REG_R,))
            em("scalar", "copy", out=gKC()[:, 0:512],
               in_=hKCB()[:, 4 * g * 128:(4 * g + 4) * 128],
               inc=(sca, 1), reads=(REG_B,), writes=(REG_R,))
            GCV = sca.n
            for i, c in enumerate(cs):
                em("vector", "tensor_tensor",
                   out=dKBG()[:, i * 128:(i + 1) * 128],
                   in0=gKC()[:, i * 128:(i + 1) * 128],
                   in1=hBETA()[:, c * 128:(c + 1) * 128], op=Alu.mult,
                   wait=([(sca, GCV)] if i == 0 else []) +
                        (PQm if i == 0 else []),
                   inc=(vec, 1), reads=(REG_B, REG_R), writes=(REG_R,))
                bk = banks[4 + i % 2]
                em("tensor", "transpose", bk.apt(F32),
                   hBETA()[:, c * 128:(c + 1) * 128], t_eye_f[:],
                   wait=[(vec, vec.n)] + bk.free_at, inc=(mm, 1))
                em("scalar", "copy", out=dBTG()[:, i * 128:(i + 1) * 128],
                   in_=bk.apt(F32), wait=[(mm, mm.n)], inc=(sca, 1),
                   reads=(), writes=(REG_B,))
                bk.free_at = [(sca, sca.n)]
                bk2 = banks[2 + i % 2]
                em("tensor", "transpose", bk2.apt(F32R),
                   gKC()[:, i * 128:(i + 1) * 128], t_eye_r[:],
                   wait=bk2.free_at, inc=(mm, 1))
                em("scalar", "copy", out=dKCTM()[:, i * 128:(i + 1) * 128],
                   in_=bk2.apt(F32R), wait=[(mm, mm.n)], inc=(sca, 1),
                   reads=(), writes=(REG_R,))
                bk2.free_at = [(sca, sca.n)]
                em("vector", "tensor_tensor",
                   out=dPa()[:, i * 256:i * 256 + 128],
                   in0=hVRAW()[:, c * 128:(c + 1) * 128],
                   in1=dBTG()[:, i * 128:(i + 1) * 128], op=Alu.mult,
                   wait=[(sca, sca.n)], inc=(vec, 1),
                   reads=(REG_B, REG_R), writes=(REG_R,))
                em("vector", "tensor_tensor",
                   out=dPa()[:, i * 256 + 128:i * 256 + 256],
                   in0=dKCTM()[:, i * 128:(i + 1) * 128],
                   in1=dBTG()[:, i * 128:(i + 1) * 128], op=Alu.mult,
                   inc=(vec, 1), reads=(REG_B, REG_R), writes=(REG_R,))
            PREP = vec.n
            if g == NCH // 4 - 1:
                VLAST = PREP
            for i, c in enumerate(cs):
                em("tensor", "matmul", banks[4].ap(128, off=i * 128),
                   dKBG()[:, i * 128:(i + 1) * 128],
                   gKC()[:, i * 128:(i + 1) * 128],
                   wait=([(vec, PREP)] + banks[4].free_at) if i == 0 else [],
                   start=True, stop=True, inc=(mm, 1) if i == 3 else None)
                em("tensor", "matmul", banks[5].ap(128, off=i * 128),
                   gKC()[:, i * 128:(i + 1) * 128],
                   dKBG()[:, i * 128:(i + 1) * 128],
                   wait=banks[5].free_at if i == 0 else [],
                   start=True, stop=True, inc=(mm, 1) if i == 3 else None)
            AMM = mm.n
            mlb = t_mlneg[:].unsqueeze(1).broadcast_to([128, 4, 128])
            mub = t_muneg[:].unsqueeze(1).broadcast_to([128, 4, 128])
            em("vector", "tensor_tensor",
               out=dQ(0)[:, 0:512].rearrange("p (g f) -> p g f", f=128),
               in0=banks[4].ap(512).rearrange("p (g f) -> p g f", f=128),
               in1=mlb, op=Alu.mult, wait=[(mm, AMM)],
               inc=(vec, 1), reads=(t_mlneg, REG_R), writes=(REG_R,))
            em("vector", "tensor_tensor",
               out=dQ(1)[:, 0:512].rearrange("p (g f) -> p g f", f=128),
               in0=banks[5].ap(512).rearrange("p (g f) -> p g f", f=128),
               in1=mub, op=Alu.mult,
               inc=(vec, 1), reads=(t_muneg,), writes=(REG_R,))
            banks[4].free_at = [(vec, vec.n)]
            banks[5].free_at = [(vec, vec.n)]
            P, Pn_, Q, QT = dPa(), dPb(), dQ(0), dQ(1)
            for it in range(7):
                last = (it == 6)
                PW = vec.n
                SCD = sca.n
                if not last:
                    for i in range(4):
                        em("tensor", "matmul", banks[2].ap(128, off=i * 128),
                           QT[:, i * 128:(i + 1) * 128],
                           Q[:, i * 128:(i + 1) * 128],
                           wait=(banks[2].free_at + [(vec, PW), (sca, SCD)])
                           if i == 0 else [],
                           start=True, stop=True,
                           inc=(mm, 1) if i == 3 else None)
                        em("tensor", "matmul", banks[3].ap(128, off=i * 128),
                           Q[:, i * 128:(i + 1) * 128],
                           QT[:, i * 128:(i + 1) * 128],
                           wait=(banks[3].free_at + [(vec, PW), (sca, SCD)])
                           if i == 0 else [],
                           start=True, stop=True,
                           inc=(mm, 1) if i == 3 else None)
                for i in range(4):
                    bk = banks[i % 2]
                    em("tensor", "matmul", bk.ap(256, off=(i // 2) * 256),
                       QT[:, i * 128:(i + 1) * 128],
                       P[:, i * 256:(i + 1) * 256],
                       wait=([(vec, PW), (sca, SCD)] + bk.free_at)
                       if i < 2 else [],
                       start=True, stop=True,
                       inc=(mm, 1) if i >= 2 else None)
                PMM = mm.n
                QMM = PMM
                for half in range(2):
                    em("vector", "scalar_tensor_tensor",
                       out=Pn_[:, half * 512:(half + 1) * 512],
                       in0=banks[half].ap(512), scalar=0.0,
                       in1=P[:, half * 512:(half + 1) * 512],
                       op0=Alu.add, op1=Alu.add, wait=[(mm, PMM)],
                       inc=(vec, 1), reads=(REG_R,), writes=(REG_R,))
                    banks[half].free_at = [(vec, vec.n)]
                if not last:
                    em("scalar", "copy", out=Q[:, 0:512],
                       in_=banks[2].ap(512), wait=[(mm, QMM)], inc=(sca, 1),
                       reads=(), writes=(REG_R,))
                    banks[2].free_at = [(sca, sca.n)]
                    em("scalar", "copy", out=QT[:, 0:512],
                       in_=banks[3].ap(512), wait=[(mm, QMM)], inc=(sca, 1),
                       reads=(), writes=(REG_R,))
                    banks[3].free_at = [(sca, sca.n)]
                    P, Pn_ = Pn_, P
            UWB = Pn_
            UWD = vec.n
            for i, c in enumerate(cs):
                if c > 0:
                    bk = banks[2]
                    em("tensor", "transpose", bk.apt(F32R),
                       UWB[:, i * 256 + 128:(i + 1) * 256], t_eye_r[:],
                       wait=([(vec, UWD)] if i == 0 else []) + bk.free_at,
                       inc=(mm, 1))
                    em("scalar", "copy", out=t_wT[:], in_=bk.apt(F32R),
                       wait=[(mm, mm.n)], inc=(sca, 1), reads=(),
                       writes=(t_wT,))
                    bk.free_at = [(sca, sca.n)]
                    bk = banks[3]
                    em("tensor", "matmul", bk.ap(128), t_wT[:], t_S[:],
                       wait=[(sca, sca.n), (vec, SAD)] + bk.free_at,
                       start=True, stop=True, inc=(mm, 1))
                    em("vector", "scalar_tensor_tensor", out=t_up[:],
                       in0=bk.ap(128), scalar=-1.0,
                       in1=UWB[:, i * 256:i * 256 + 128],
                       op0=Alu.mult, op1=Alu.add, wait=[(mm, mm.n)],
                       inc=(vec, 1), reads=(REG_R, t_S), writes=(t_up,))
                    bk.free_at = [(vec, vec.n)]
                    up_ap = t_up[:]
                    UPD = vec.n
                else:
                    up_ap = UWB[:, 0:128]
                    UPD = UWD
                bk = banks[4 + (i % 2)]
                em("tensor", "matmul", bk.ap(128, off=256),
                   gKC()[:, i * 128:(i + 1) * 128],
                   gQS()[:, i * 128:(i + 1) * 128],
                   wait=bk.free_at + [(vec, UPD)],
                   start=True, stop=True, inc=(mm, 1))
                em("vector", "tensor_tensor", out=dKBG()[:, 0:128],
                   in0=bk.ap(128, off=256), in1=t_muincl[:], op=Alu.mult,
                   wait=[(mm, mm.n)], inc=(vec, 1),
                   reads=(t_muincl, REG_R), writes=(REG_R,))
                bk.free_at = [(vec, vec.n)]
                ATD = vec.n
                if c > 0:
                    em("tensor", "matmul", bko.ap(128), t_S[:],
                       gQS()[:, i * 128:(i + 1) * 128],
                       wait=bko.free_at + [(vec, ATD)], start=True,
                       stop=False)
                    em("tensor", "matmul", bko.ap(128), up_ap,
                       dKBG()[:, 0:128], start=False, stop=True, inc=(mm, 1))
                else:
                    em("tensor", "matmul", bko.ap(128), up_ap,
                       dKBG()[:, 0:128],
                       wait=[(vec, ATD)] + bko.free_at, start=True,
                       stop=True, inc=(mm, 1))
                em("vector", "tensor_copy",
                   out=attnpre(h)[:, c * 128:(c + 1) * 128],
                   in_=bko.ap(128),
                   wait=[(mm, mm.n)], inc=(vec, 1), reads=(),
                   writes=(REG_A,))
                bko.free_at = [(vec, vec.n)]
                em("tensor", "matmul", bkS.ap(128),
                   dKCTM()[:, i * 128:(i + 1) * 128], up_ap,
                   wait=[(vec, UPD)] + bkS.free_at, start=True, stop=True,
                   inc=(mm, 1))
                if c < NCH - 1:
                    if c == 0:
                        em("vector", "tensor_copy", out=t_S[:],
                           in_=bkS.ap(128), wait=[(mm, mm.n)], inc=(vec, 1),
                           reads=(), writes=(t_S,))
                    else:
                        em("vector", "tensor_tensor", out=t_S[:], in0=t_S[:],
                           in1=bkS.ap(128), op=Alu.add, wait=[(mm, mm.n)],
                           inc=(vec, 1), reads=(t_S,), writes=(t_S,))
                    bkS.free_at = [(vec, vec.n)]
                    SAD = vec.n
                else:
                    bkS.free_at = [(mm, mm.n)]
            PQm = [(vec, vec.n), (sca, sca.n)]
        prev_head = [(vec, vec.n), (sca, sca.n), (mm, mm.n)]
    o2 = 26624

    def q2(st): return vB(o2 + st * 19456, BF16, N)

    def k2(st): return vB(o2 + st * 19456 + 4096, BF16, N)

    def v2(st): return vB(o2 + st * 19456 + 8192, BF16, N)

    def rowc2(st): return vB(o2 + st * 19456 + 12288, BF16, N)

    def rowd2(st): return vB(o2 + st * 19456 + 16384, BF16, 512)
    p2m = [list(prev_head) + [(gpA, gpA.n)],
           list(prev_head) + [(gpA, gpA.n)]]
    lds2 = [ldh, ld0]
    for h in range(H):
        st_ = h % 2
        lane2 = lds2[st_]
        em("sync", "dma_start", out=q2(st_), in_=qraw_s[h, :, :],
           wait=p2m[st_], inc=(lane2, 16))
        em("sync", "dma_start", out=k2(st_), in_=kraw_s[h, :, :],
           inc=(lane2, 16))
        em("sync", "dma_start", out=rowc2(st_)[0:1, 0:N],
           in_=bound_s[h, :, :], inc=(lane2, 16))
        for c in range(TT):
            em("sync", "dma_start",
               out=v2(st_)[:, c * 128:(c + 1) * 128],
               in_=vraw_s[c * 128:(c + 1) * 128, h * 128:(h + 1) * 128],
               inc=(lane2, 16))
        LDH2 = lane2.n
        for s in range(NS):
            nk = (s + 1) * 4
            bk_sum = banks[4]
            bk_o = banks[5]
            ptm = [[], []]
            pendc = None
            for kc in range(nk + 1):
                if kc < nk:
                    pb = kc % 2
                    bk = banks[pb]
                    PT = t_bfa if pb == 0 else t_bfb
                    em("tensor", "matmul", bk.ap(512),
                       k2(st_)[:, kc * 128:(kc + 1) * 128],
                       q2(st_)[:, s * 512:(s + 1) * 512],
                       wait=bk.free_at + ([(lane2, LDH2)] if kc == 0
                                          and s == 0 else []),
                       start=True, stop=False)
                    em("tensor", "matmul", bk.ap(512), t_ones_col_bf[:],
                       rowc2(st_)[0:1, s * 512:(s + 1) * 512],
                       start=False, stop=True, inc=(mm, 1))
                    em("scalar", "activation", out=PT[:, 0:512],
                       in_=bk.ap(512),
                       func=ACT.Exp, scale=ISQ, wait=[(mm, mm.n)] + ptm[pb],
                       inc=(sca, 1), reads=(), writes=(PT,))
                    bk.free_at = [(sca, sca.n)]
                    if kc >= nk - 4:
                        em("vector", "tensor_tensor", out=PT[:, 0:512],
                           in0=PT[:, 0:512],
                           in1=t_dmaskv()[:, (kc - (nk - 4)) * 512:
                                          (kc - (nk - 4) + 1) * 512],
                           op=Alu.mult,
                           wait=[(sca, sca.n)] + ([(ldc, DMLD)]
                                                  if h == 0 and s == 0 and
                                                  kc == 0 else []),
                           inc=(vec, 1), reads=(PT,), writes=(PT,))
                        PTD = [(vec, vec.n)]
                    else:
                        PTD = [(sca, sca.n)]
                    pend_new = (kc, PT, PTD)
                else:
                    pend_new = None
                if pendc is not None:
                    (kp, PTp, PTDp) = pendc
                    em("tensor", "matmul", bk_sum.ap(512)[0:1, :],
                       t_ones128_bf[:], PTp[:, 0:512],
                       wait=PTDp + (bk_sum.free_at if kp == 0 else []),
                       start=(kp == 0), stop=(kp == nk - 1), inc=(mm, 1))
                    em("tensor", "matmul", bk_o.ap(512),
                       v2(st_)[:, kp * 128:(kp + 1) * 128], PTp[:, 0:512],
                       wait=(bk_o.free_at if kp == 0 else []),
                       start=(kp == 0), stop=(kp == nk - 1), inc=(mm, 1))
                    ptm[kp % 2] = [(mm, mm.n)]
                pendc = pend_new
            SMM = mm.n
            em("vector", "tensor_scalar", out=rowd2(st_)[0:1, 0:512],
               in0=bk_sum.ap(512)[0:1, :], scalar1=2.0, scalar2=None,
               op0=Alu.mult, wait=[(mm, SMM)], inc=(vec, 1),
               reads=(), writes=(REG_B,))
            bk_sum.free_at = [(vec, vec.n)]
            bkb = banks[2]
            em("tensor", "matmul", bkb.ap(512), t_ones_col_bf[:],
               rowd2(st_)[0:1, 0:512], wait=[(vec, vec.n)] + bkb.free_at,
               start=True, stop=True, inc=(mm, 1))
            em("vector", "reciprocal", out=t_w2[:, 1024:1536],
               in_=bkb.ap(512), wait=[(mm, mm.n)], inc=(vec, 1),
               reads=(t_w2,), writes=(t_w2,))
            bkb.free_at = [(vec, vec.n)]
            em("vector", "tensor_tensor", out=t_w2[:, 1536:2048],
               in0=bk_o.ap(512), in1=t_w2[:, 1024:1536], op=Alu.mult,
               reads=(t_w2,), writes=(t_w2,))
            bk_o.free_at = [(vec, vec.n + 1)]
            em("vector", "tensor_tensor",
               out=attnpre(h)[:, s * 512:(s + 1) * 512],
               in0=attnpre(h)[:, s * 512:(s + 1) * 512],
               in1=t_w2[:, 1536:2048],
               op=Alu.add, inc=(vec, 1), reads=(t_w2, REG_A),
               writes=(REG_A,))
        p2m[st_] = [(vec, vec.n), (sca, sca.n), (mm, mm.n)]
    run_block()
    if stop_after == 'B3':
        st.close()
        return nc

    new_block()
    GPB2 = gp_all()
    for srcrow, dstf in [(lnw, lnw_rep), (lnb, lnb_rep)]:
        for j in range(D // 512):
            em("sync", "dma_start", out=t_rows[0:1, 0:512],
               in_=srcrow[0:1, j * 512:(j + 1) * 512],
               wait=[(vec, vec.n), (mm, mm.n)], inc=(ldc, 16))
            bk = banks[6]
            em("tensor", "matmul", bk.ap(512), t_ones_col[:],
               t_rows[0:1, 0:512], wait=[(ldc, ldc.n)] + bk.free_at,
               start=True, stop=True, inc=(mm, 1))
            em("vector", "tensor_copy",
               out=dstf()[:, j * 512:(j + 1) * 512], in_=bk.ap(512),
               wait=[(mm, mm.n)], inc=(vec, 1), reads=(), writes=(REG_B,))
            bk.free_at = [(vec, vec.n)]
            ewait("sync", (vec, vec.n))
    wsm = [[], []]
    xm = []
    pend4 = []
    TR4 = {}

    def b4_flush():
        (t_, QD_) = pend4.pop(0)
        dst_ = t_bfb if t_ % 2 == 0 else t_bfa
        for k in range(KD):
            bk_ = banks[4 + (k % 2)]
            em("tensor", "transpose", bk_.apt(BF16),
               dst_[:, k * 128:(k + 1) * 128], t_eye_bf[:],
               wait=([(poo, QD_)] if k < 2 else []) + bk_.free_at,
               inc=(mm, 1))
            em("scalar", "copy", out=tqT(k)[:, t_ * 128:(t_ + 1) * 128],
               in_=bk_.apt(BF16), wait=[(mm, mm.n)], inc=(sca, 1))
            bk_.free_at = [(sca, sca.n)]
        TR4[t_] = mm.n
    for t in range(TT):
        em("sync", "dma_start", out=t_w0[:, 0:D],
           in_=x_in[t * 128:(t + 1) * 128, :], wait=xm, inc=(ld0, 16))
        XLD = ld0.n
        for k in range(KD):
            b = k % 2
            lw = ld2 if b == 0 else ld3
            em("sync", "dma_start", out=wo_slab(b),
               in_=wobf_s[k * 128:(k + 1) * 128, :],
               wait=wsm[b] + (GPB2 if t == 0 and k == 0 else []),
               inc=(lw, 16))
            WLD = (lw, lw.n)
            for j in range(D // 512):
                bk = banks[j]
                em("tensor", "matmul", bk.ap(512),
                   attnpre(k)[:, t * 128:(t + 1) * 128],
                   wo_slab(b)[:, j * 512:(j + 1) * 512],
                   wait=([WLD] if j == 0 else []) +
                        (bk.free_at if k == 0 else []),
                   start=(k == 0), stop=(k == KD - 1),
                   inc=(mm, 1) if j == D // 512 - 1 else None)
            wsm[b] = [(mm, mm.n)]
        WOMM = mm.n
        if pend4:
            b4_flush()
        for j in range(D // 512):
            bk = banks[j]
            em("vector", "scalar_tensor_tensor",
               out=t_w2[:, j * 512:(j + 1) * 512], in0=bk.ap(512),
               scalar=0.0, in1=t_w0[:, j * 512:(j + 1) * 512],
               op0=Alu.add, op1=Alu.add,
               wait=[(mm, WOMM), (ld0, XLD)] +
                    ([(poo, poo.n)] if j == 0 else []),
               inc=(vec, 1), reads=(t_w0,), writes=(t_w2,))
            bk.free_at = [(vec, vec.n)]
        xm = [(vec, vec.n)]
        layer_norm_apply(t_w2[:, 0:D], h1(t), inc=(sca, 1),
                         extra_reads=(t_w2,))
        layer_norm_apply(h1(t), t_w1[:, 0:D], inc=(sca, 1),
                         extra_reads=(REG_B, t_w1), wait_src=[(sca, sca.n)])
        srcq = t_bfa if t % 2 == 0 else t_bfb
        dstq = t_bfb if t % 2 == 0 else t_bfa
        em("vector", "tensor_tensor", out=t_w1[:, 0:D], in0=t_w1[:, 0:D],
           in1=lnw_rep(), op=Alu.mult, wait=[(sca, sca.n)],
           reads=(t_w1, REG_B), writes=(t_w1,))
        em("vector", "tensor_tensor", out=srcq[:, 0:D], in0=t_w1[:, 0:D],
           in1=lnb_rep(), op=Alu.add, inc=(vec, 1),
           wait=([(mm, TR4[t - 2])] if t - 2 in TR4 else []),
           reads=(t_w1,), writes=(srcq,))
        quant_tile(srcq[:, 0:D], D, dstq[:, 0:D], inc=(poo, 1),
                   wait_src=[(vec, vec.n)], src_reads=(srcq,),
                   dst_wait=([(mm, TR4[t - 1])] if t - 1 in TR4 else []))
        pend4.append((t, poo.n))
    while pend4:
        b4_flush()
    run_block()
    if stop_after == 'B4':
        st.close()
        return nc

    new_block()
    gql = [gpA, gpB]
    gqslot = [[], []]
    w1m = [gp_all(), gp_all()]
    SLDv = [0, 0]
    b1m = []

    def w1sl(sbuf, k):
        if sbuf == 0:
            return w1slab()[:, k * 512:(k + 1) * 512]
        tt_ = t_w0 if k < KD // 2 else t_w1
        kk_ = k % (KD // 2)
        return tt_[:].bitcast(BF16)[:, kk_ * 512:(kk_ + 1) * 512]

    b1c = t_w2[:, 512:1024]
    TRMMv = {}

    def b5_dst(t):
        return t_bfb[:, 512 + (t % 2) * 512:1024 + (t % 2) * 512]

    def b5_flush(pend):
        (t, QD, EVD, slot, bk) = pend.pop(0)
        for i in range(4):
            em("tensor", "transpose", bk.apt(BF16, 128, 64 * i),
               b5_dst(t)[:, i * 128:(i + 1) * 128], t_eye_bf[:],
               wait=(([(poo, QD), (vec, EVD)] + bk.free_at) if i == 0
                     else []), inc=(mm, 1))
        TRMMv[t] = mm.n
        stg = t_bfa[:, 512 + slot * 512:1024 + slot * 512]
        em("scalar", "copy", out=stg, in_=bk.apt(BF16, 512, 0),
           wait=[(mm, mm.n)] + gqslot[slot], inc=(sca, 1),
           reads=(), writes=(t_bfa,))
        bk.free_at = [(sca, sca.n)]
        lane_g = gql[slot]
        fc_, t_ = t // TT, t % TT
        for i in range(4):
            em("gpsimd", "dma_start",
               out=gqT_s[fc_ * 512 + i * 128:fc_ * 512 + (i + 1) * 128,
                         t_ * 128:(t_ + 1) * 128],
               in_=stg[:, i * 128:(i + 1) * 128],
               wait=([(sca, sca.n)] if i == 0 else []),
               inc=(lane_g, 16))
        gqslot[slot] = [(lane_g, lane_g.n)]
    pend5 = []
    for fc in range(FD):
        sb = fc % 2
        lane_s = ld2 if sb == 0 else ld3
        em("sync", "dma_start", out=t_rows[0:1, 0:512],
           in_=b1[0:1, fc * 512:(fc + 1) * 512], wait=b1m, inc=(ldc, 16))
        B1LD = ldc.n
        bk0 = banks[0]
        em("tensor", "matmul", bk0.ap(512), t_ones_col[:],
           t_rows[0:1, 0:512], wait=[(ldc, B1LD)] + bk0.free_at,
           start=True, stop=True, inc=(mm, 1))
        b1m = [(mm, mm.n)]
        em("vector", "tensor_copy", out=b1c, in_=bk0.ap(512),
           wait=[(mm, mm.n)], inc=(vec, 1), reads=(), writes=(t_w2,))
        bk0.free_at = [(vec, vec.n)]
        for k in range(KD):
            em("sync", "dma_start", out=w1sl(sb, k),
               in_=w1qt_s[k * 128:(k + 1) * 128, fc * 512:(fc + 1) * 512],
               wait=(w1m[sb] if k == 0 else []), inc=(lane_s, 16))
        SLDv[sb] = lane_s.n
        for t in range(TT):
            gt = fc * TT + t
            bk = banks[t % 8]
            for k in range(KD):
                em("tensor", "matmul", bk.ap(512),
                   tqT(k)[:, t * 128:(t + 1) * 128], w1sl(sb, k),
                   wait=([(lane_s, SLDv[sb])] if k == 0 else []) +
                        (bk.free_at if k == 0 else []),
                   start=(k == 0), stop=(k == KD - 1),
                   inc=(mm, 1) if k == KD - 1 else None)
            MMD = mm.n
            if len(pend5) >= 2:
                b5_flush(pend5)
            em("vector", "scalar_tensor_tensor", out=t_bfa[:, 0:512],
               in0=bk.ap(512), scalar=0.0, in1=b1c, op0=Alu.add,
               op1=Alu.add, wait=[(mm, MMD)], inc=(vec, 1),
               reads=(t_w2,), writes=(t_bfa,))
            EVD = vec.n
            em("scalar", "activation", out=t_bfb[:, 0:512],
               in_=t_bfa[:, 0:512], func=ACT.Gelu_apprx_tanh,
               wait=[(vec, EVD)], inc=(sca, 1),
               reads=(t_bfa,), writes=(t_bfb,))
            dwt = ([(mm, TRMMv[gt - 2])] if gt - 2 in TRMMv else [])
            quant_tile(t_bfb[:, 0:512], 512, b5_dst(gt),
                       inc=(poo, 1), wait_src=[(sca, sca.n)],
                       src_reads=(t_bfb,), dst_wait=dwt)
            pend5.append((gt, poo.n, EVD, gt % 2, bk))
        while pend5:
            b5_flush(pend5)
        w1m[sb] = [(mm, mm.n)]
    run_block()
    if stop_after == 'B5':
        st.close()
        return nc

    new_block()
    GPB6 = gp_all()
    for j in range(D // 512):
        em("sync", "dma_start", out=t_rows[0:1, 0:512],
           in_=b2[0:1, j * 512:(j + 1) * 512],
           wait=[(vec, vec.n)] + (GPB6 if j == 0 else []), inc=(ldc, 16))
        em("vector", "tensor_copy", out=b2row()[0:1, j * 512:(j + 1) * 512],
           in_=t_rows[0:1, 0:512], wait=[(ldc, ldc.n)], inc=(vec, 1),
           reads=(), writes=(REG_B,))
        ewait("sync", (vec, vec.n))
    B2RD = vec.n
    t_w2b = t_w2[:].bitcast(BF16)

    def gqT8q():
        return vA(0, BF16, KF * 512)

    def stage6(i):
        return [t_bfa, t_bfb][i] if i < 2 else \
            t_w2b[:, (i - 2) * 2048:(i - 1) * 2048]
    lgq = [ldh, ld0]
    lpz = [ld2, ld3, ldc, ld1]
    qm = list(GPB6)
    w2m = [[], [], [], []]

    def w2p(slot):
        return vB(TAIL + slot * 1024, BF16, 512)
    for qd in range(TT // 4):
        for f in range(KF):
            lg = lgq[f % 2]
            em("sync", "dma_start",
               out=gqT8q()[:, f * 512:(f + 1) * 512],
               in_=gqT_s[f * 128:(f + 1) * 128,
                         qd * 512:(qd + 1) * 512],
               wait=((qm + [(ldh, ldh.n), (ld0, ld0.n)]) if f < 2 else []),
               inc=(lg, 16))
        GQD = [(ldh, ldh.n), (ld0, ld0.n)]
        for j in range(D // 512):
            bs = (j % 2) * 4
            for f in range(KF):
                b = f % 4
                lw = lpz[b]
                em("sync", "dma_start", out=w2p(b),
                   in_=w2qt_s[f * 128:(f + 1) * 128,
                              j * 512:(j + 1) * 512],
                   wait=w2m[b], inc=(lw, 16))
                WLD = (lw, lw.n)
                for i in range(4):
                    bk = banks[bs + i]
                    em("tensor", "matmul", bk.ap(512),
                       gqT8q()[:, f * 512 + i * 128:
                               f * 512 + (i + 1) * 128],
                       w2p(b),
                       wait=([WLD] if i == 0 else []) +
                            (GQD if f == 0 and i == 0 and j == 0 else []) +
                            (bk.free_at if f == 0 else []),
                       start=(f == 0), stop=False,
                       inc=(mm, 1) if i == 3 else None)
                w2m[b] = [(mm, mm.n)]
            for i in range(4):
                bk = banks[bs + i]
                em("tensor", "matmul", bk.ap(512), t_ones_col[:],
                   b2row()[0:1, j * 512:(j + 1) * 512],
                   wait=([(vec, B2RD)] if qd == 0 and j == 0 and i == 0
                         else []),
                   start=False, stop=True, inc=(mm, 1) if i == 3 else None)
            YMM = mm.n
            for i in range(4):
                bk = banks[bs + i]
                em("vector", "tensor_copy",
                   out=stage6(i)[:, j * 512:(j + 1) * 512], in_=bk.ap(512),
                   wait=[(mm, YMM)], inc=(vec, 1),
                   reads=(), writes=(t_bfa, t_bfb, t_w2))
                bk.free_at = [(vec, vec.n)]
        for i in range(4):
            t = 4 * qd + i
            em("vector", "tensor_tensor", out=stage6(i)[:, 0:D],
               in0=stage6(i)[:, 0:D], in1=h1(t), op=Alu.add,
               reads=(t_bfa, t_bfb, t_w2, REG_B),
               writes=(t_bfa, t_bfb, t_w2))
            em("vector", "tensor_tensor", out=t_w0[:, 0:D],
               in0=stage6(i)[:, 0:D], in1=h1(t), op=Alu.add, inc=(vec, 1),
               wait=[(sca, sca.n)],
               reads=(t_bfa, t_bfb, t_w2), writes=(t_w0,))
            layer_norm_apply(t_w0[:, 0:D], t_w1[:, 0:D], inc=(sca, 1),
                             extra_reads=(t_w0, t_w1),
                             act_wait=[(out_c, out_c.n)])
            em("sync", "dma_start", out=y_out[t * 128:(t + 1) * 128, :],
               in_=t_w1[:, 0:D], wait=[(sca, sca.n)], inc=(out_c, 16))
        qm = [(mm, mm.n)]
    ewait("sync", (out_c, out_c.n))
    run_block()
    if stop_after == 'B6':
        st.close()
        return nc

    st.close()
    nc._dbg_ctrs = {k: v.n for k, v in [("ldc", ldc), ("ld0", ld0),
        ("ld1", ld1), ("ld2", ld2), ("ld3", ld3), ("ldh", ldh), ("mm", mm),
        ("vec", vec), ("sca", sca), ("gpA", gpA), ("gpB", gpB),
        ("gpC", gpC), ("gpD", gpD), ("out", out_c)]}
    return nc


_NC_CACHE = {}


def kernel(L0_attn_x, L0_attn_Wq, L0_attn_Wk, L0_attn_Wv, L0_attn_Wo,
           L0_ffn_ln_w, L0_ffn_ln_b, L0_ffn_W1, L0_ffn_b1, L0_ffn_W2,
           L0_ffn_b2):
    cfg = Cfg(N=2048, D=2048, H=16, DFF=8192)
    if "full" not in _NC_CACHE:
        _NC_CACHE["full"] = build_program(cfg)
    nc = _NC_CACHE["full"]
    cst = _consts(cfg)
    shared = dict(
        Wq=np.ascontiguousarray(L0_attn_Wq, np.float32),
        Wk=np.ascontiguousarray(L0_attn_Wk, np.float32),
        Wv=np.ascontiguousarray(L0_attn_Wv, np.float32),
        Wo=np.ascontiguousarray(L0_attn_Wo, np.float32),
        lnw=np.ascontiguousarray(L0_ffn_ln_w, np.float32).reshape(1, -1),
        lnb=np.ascontiguousarray(L0_ffn_ln_b, np.float32).reshape(1, -1),
        W1=np.ascontiguousarray(L0_ffn_W1, np.float32),
        b1=np.ascontiguousarray(L0_ffn_b1, np.float32).reshape(1, -1),
        W2=np.ascontiguousarray(L0_ffn_W2, np.float32),
        b2=np.ascontiguousarray(L0_ffn_b2, np.float32).reshape(1, -1),
        **cst,
    )
    x = np.ascontiguousarray(L0_attn_x, np.float32)
    in_maps = [dict(shared, x_in=x[i]) for i in range(8)]
    res = run_bass_kernel_spmd(nc, in_maps, core_ids=list(range(8)))
    out = np.stack([np.asarray(r["y_out"], np.float32) for r in res.results])
    return out
